# version 15
# baseline (speedup 1.0000x reference)
import numpy as np
import ml_dtypes
import bisect
import contextlib
import concourse.bass as bass
import concourse.mybir as mybir
from concourse.bass_utils import run_bass_kernel_spmd
from concourse.bass_isa import ReduceOp

F32 = mybir.dt.float32
BF16 = mybir.dt.bfloat16
AL = mybir.AluOpType
ACT = mybir.ActivationFunctionType
AXX = mybir.AxisListType.X
bf16 = ml_dtypes.bfloat16

NCOR = 8
C = 1024
T_ALL = 16384
TC = T_ALL // NCOR
TS = 256
TP1 = TS + 1
NT = TC // TS
H = 16
L = 64
CPT = TS // L
NCH = TC // L
PCH = 8
NP8 = 8
FT = PCH * TS
FT1 = PCH * TP1
LN1_EPS = 1e-5
GN_EPS = 1e-5 * 64.0
LN2_EPS = 1e-3
FFN_EPS = 1e-5
QB = 128.0
MAGIC = float(np.float32(1.5 * 2.0 ** 23))
DEBUG = False
PHASES = []
EV = slice(0, 64)
OD = slice(64, 128)
TPE = (0, 0)
TPO = (64, 64)
NSOLVE = 6


class Sem:
    def __init__(self, h, ooo=False):
        self.h = h
        self.v = 0
        self.ooo = ooo
        self.km = []
        self.kd = []


class Op:
    __slots__ = ("fn", "incs", "iswait")

    def __init__(self, fn, iswait=False):
        self.fn = fn
        self.incs = []
        self.iswait = iswait


class Eng:

    def __init__(self, name, sem, dinc=1):
        self.name = name
        self.q = []
        self.sem = sem
        self.dinc = dinc
        self.waited = {}

    def op(self, fn, inc=None, sem=None):
        o = Op(fn)
        self.q.append(o)
        s = self.sem if sem is None else sem
        n = self.dinc if inc is None else inc
        o.incs.append((s, n))
        s.v += n
        if self.waited:
            s.km.append(s.v)
            s.kd.append(dict(self.waited))
        return s.v

    def chk(self, sem, n=None):
        return sem.v

    def wait(self, sem, v=None):
        val = sem.v if v is None else v
        if val <= 0:
            return
        k = id(sem)
        if self.waited.get(k, 0) >= val:
            return
        self.waited[k] = val
        self.q.append(Op(lambda e, s=sem, vv=val: e.wait_ge(s.h, vv),
                         iswait=True))
        if sem.km:
            i = bisect.bisect_right(sem.km, val) - 1
            if i >= 0:
                for kk, vv in sem.kd[i].items():
                    if self.waited.get(kk, 0) < vv:
                        self.waited[kk] = vv

    def run(self, e):
        for o in self.q:
            r = o.fn(e)
            for (sem, n) in o.incs:
                r.then_inc(sem.h, n)


def build_program():
    PHASES.clear()
    nc = bass.Bass("TRN2", target_bir_lowering=False,
                   dynamic_dma_scratch_size=4096)

    def I(name, shape, dtype=F32):
        return nc.dram_tensor(name, list(shape), dtype, kind="ExternalInput")

    xT_h = I("xT", (C, TC + 1))
    sel_h = I("sel", (1, 8))
    W_h = {k: I(k, (C, C)) for k in ("W_r", "W_k", "W_v", "W_o", "ffn_w")}
    maa1_h = I("maa_w1", (C, 192))
    maa2_h = I("maa_w2", (6, 32, C))
    d1_h = I("decay_w1", (C, 64)); d2_h = I("decay_w2", (64, C))
    a1_h = I("aaa_w1", (C, 64)); a2_h = I("aaa_w2", (64, C))
    k1_h = I("kkk_w1", (C, 64)); k2_h = I("kkk_w2", (64, C))
    g1_h = I("gate_w1", (C, 128)); g2_h = I("gate_w2", (128, C))
    PNAMES = ("maa_x", "maa_r", "maa_w", "maa_k", "maa_v", "maa_a", "maa_g",
              "ndecay", "aaaaa", "faaaa", "ln_w", "ln_b", "ffn_g", "ffn_b")
    p_h = {nm: I("p_" + nm, (128, PCH)) for nm in PNAMES}
    idn_h = I("idn", (128, 128))
    msu_h = I("msk_su", (128, 128))
    msl_h = I("msk_sl", (128, 64))
    hind_h = I("hind", (128, PCH * 16))
    hexp_h = I("hexp", (16, PCH * 128))
    cval_h = I("cvals", (128, 8))

    out_h = nc.dram_tensor("out", [C, TC], F32, kind="ExternalOutput")

    def DD(name, shape, dtype=BF16):
        kind = "ExternalOutput" if DEBUG else "Internal"
        return nc.dram_tensor(name, list(shape), dtype, kind=kind)

    vsp_h = DD("v_sp", (C, TC))
    gsp_h = DD("g_sp", (C, TC))
    ylsp_h = DD("yl_sp", (C, TC))
    ycsp_h = DD("yc_sp", (128, NP8 * TC))
    x1sp_h = DD("x1_sp", (C, TC), F32)
    xnsp_h = DD("xn_sp", (C, TC))
    dbg_h = {}
    if DEBUG:
        dbg_h["d_zin"] = nc.dram_tensor("d_zin", [128, NP8 * 64], F32,
                                        kind="ExternalOutput")
        dbg_h["d_gcol"] = nc.dram_tensor("d_gcol", [128, 8], F32,
                                         kind="ExternalOutput")
        dbg_h["d_gc1"] = nc.dram_tensor("d_gc1", [128, 8], F32,
                                        kind="ExternalOutput")
        dbg_h["d_gc2"] = nc.dram_tensor("d_gc2", [128, 8], F32,
                                        kind="ExternalOutput")
        dbg_h["d_gc3"] = nc.dram_tensor("d_gc3", [128, 8], F32,
                                        kind="ExternalOutput")
        dbg_h["d_gmx"] = nc.dram_tensor("d_gmx", [128, NT + 10], F32,
                                        kind="ExternalOutput")
        dbg_h["d_wq"] = nc.dram_tensor("d_wq", [128, 1024], BF16,
                                       kind="ExternalOutput")
        dbg_h["d_xq"] = nc.dram_tensor("d_xq", [128, 2048], BF16,
                                       kind="ExternalOutput")
        for nm in ("d_xln", "d_r", "d_kq", "d_kk", "d_b", "d_ld", "d_af",
                   "d_eng", "d_ycr", "d_ygn", "d_yfin", "d_x32", "d_ot"):
            dt = F32 if nm in ("d_ld", "d_x32", "d_ot") else BF16
            dbg_h[nm] = nc.dram_tensor(nm, [C, TC], dt, kind="ExternalOutput")
    ag1_in = nc.dram_tensor("ag1_in", [128, NP8 * 128], F32, kind="Internal")
    ag1_out = nc.dram_tensor("ag1_out", [NCOR, 128, NP8 * 128], F32,
                             kind="Internal")
    ag2_in = nc.dram_tensor("ag2_in", [128, 1], F32, kind="Internal")
    ag2_out = nc.dram_tensor("ag2_out", [NCOR, 128, 1], F32, kind="Internal")

    ctx = contextlib.ExitStack()

    def SB(name, p, f, dtype=BF16):
        return ctx.enter_context(nc.sbuf_tensor(name, [p, f], dtype))

    def SEM(name, ooo=False):
        return Sem(ctx.enter_context(nc.semaphore(name)), ooo=ooo)

    with ctx:
        w_r = SB("w_r", 128, PCH * C)
        w_k = SB("w_k", 128, PCH * C)
        w_v = SB("w_v", 128, PCH * C)
        w_m1 = SB("w_m1", 128, PCH * 192)
        w_m2 = SB("w_m2", 128, 2 * C)
        w_d1 = SB("w_d1", 128, PCH * 64)
        w_d2 = SB("w_d2", 64, C)
        w_a1 = SB("w_a1", 128, PCH * 64)
        w_a2 = SB("w_a2", 64, C)
        w_k1 = SB("w_k1", 128, PCH * 64)
        w_k2 = SB("w_k2", 64, C)
        w_g1 = SB("w_g1", 128, PCH * 128)
        w_g2 = SB("w_g2", 128, C)
        wstg = SB("wstg", 128, C, F32)
        wstg2 = SB("wstg2", 128, C, F32)

        par = {nm: SB("pp_" + nm, 128, PCH, F32) for nm in PNAMES}
        idn = SB("idn_b", 128, 128)
        idn32 = SB("idn_f", 128, 128, F32)
        msu = SB("msu_s", 128, 128, F32)
        msl = SB("msl_s", 128, 64, F32)
        hind = SB("hind_s", 128, PCH * 16)
        hexp = SB("hexp_s", 16, PCH * 128)
        ones_r = SB("ones_r", 1, 128)
        ones_c = SB("ones_c", 128, 1)
        ones_r32 = SB("ones_r32", 1, 128, F32)
        sel128 = SB("sel128", 128, 8, F32)
        cval = SB("cval", 128, 8, F32)
        CVALS = [LN1_EPS, 1.0, -0.25, -0.5, -1.0, GN_EPS, LN2_EPS, 0.5]
        for _i, _v in enumerate(CVALS):
            nc.const_aps.register(F32, float(_v), cval[:, _i:_i + 1])

        x32 = SB("x32", 128, FT1, F32)
        ldrel = x32
        xbf = SB("xbf", 128, FT1)
        g_t = xbf
        x2bf = SB("x2bf", 128, FT1)
        epg = x2bf
        en_t = SB("en_t", 128, FT1)
        xln = SB("xln", 128, FT1)
        xx = SB("xx", 128, FT)
        dsc = SB("dsc", 128, FT)
        xs = SB("xs", 128, FT)
        xs2 = SB("xs2", 128, FT)
        ksc = SB("ksc", 128, FT)
        r_t = SB("r_t", 128, FT)
        kq_t = SB("kq_t", 128, FT)
        kk_t = SB("kk_t", 128, FT)
        b_t = SB("b_t", 128, FT)
        v_t = SB("v_t", 128, FT)
        ksc2 = SB("ksc2", 128, FT)
        r_t2 = SB("r_t2", 128, FT)
        kq_t2 = SB("kq_t2", 128, FT)
        b_t2 = SB("b_t2", 128, FT)
        v_t2 = SB("v_t2", 128, FT)
        lg_t = SB("lg_t", 128, FT, F32)
        ldg = SB("ldg", 128, FT1, F32)
        bases = SB("bases", 128, PCH * CPT, F32)
        mixs = SB("mixs", 128, 2 * TS)
        hid64 = SB("hid64", 64, TS)
        hid128 = SB("hid128", 128, TS)
        srow = SB("srow", 1, 3 * TP1, F32)
        srow2 = SB("srow2", 1, 3 * TP1, F32)
        srowb = SB("srowb", 1, 2 * TP1)
        bcA = SB("bcA", 128, TP1)
        bcB = SB("bcB", 128, TP1)
        u_bf = SB("u_bf", 16, TC)
        d63 = SB("d63", 128, PCH * NCH, F32)

        vtm = SB("vtm", 128, NP8 * 64)
        btm = SB("btm", 128, NP8 * 64)
        ktm = SB("ktm", 128, NP8 * 64)
        m_sl = SB("m_sl", 128, NP8 * 64)
        m_su = SB("m_su", 128, NP8 * 64)
        t1l = SB("t1l", 128, NP8 * 64)
        lrb = SB("lrb", 128, NP8 * 64)
        lrk = SB("lrk", 128, NP8 * 64)
        pj_a = SB("pj_a", 128, NP8 * 64)
        nj_a = SB("nj_a", 128, NP8 * 64)
        pj_b = SB("pj_b", 128, NP8 * 64)
        nj_b = SB("nj_b", 128, NP8 * 64)
        xa_t = SB("xa_t", 128, NP8 * 128)
        xb_t = SB("xb_t", 128, NP8 * 128)
        z32 = SB("z32", 128, NP8 * 128, F32)
        zbf = SB("zbf", 128, NP8 * 128)
        zin = None
        zinb = None
        agst = lg_t
        agrm = agst
        agrb = dsc
        ztmp = None
        ztmp2 = None

        y_t = kk_t
        yst = None
        cst = None
        o_t = lg_t
        xn_t = kq_t
        sqb = dsc
        gnr = ldg
        gmax = SB("gmax", 128, NT + 10, F32)
        gcol = SB("gcol", 128, 8, F32)

        psA = ctx.enter_context(nc.psum_tensor([128, 512], F32))
        psB = ctx.enter_context(nc.psum_tensor([128, 512], F32))
        psC = ctx.enter_context(nc.psum_tensor([128, 512], F32))
        psD = ctx.enter_context(nc.psum_tensor([128, 512], F32))
        psE = ctx.enter_context(nc.psum_tensor([128, 512], F32))
        psF = ctx.enter_context(nc.psum_tensor([128, 512], F32))
        psG = ctx.enter_context(nc.psum_tensor([128, 512], F32))
        psH = ctx.enter_context(nc.psum_tensor([128, 512], F32))

        s_in = SEM("s_in", ooo=True)
        s_sp = SEM("s_sp", ooo=True)
        s_mm = SEM("s_mm")
        s_ve = SEM("s_ve")
        s_sc = SEM("s_sc")
        s_cc = SEM("s_cc", ooo=True)
        s_gp = s_sp
        s_st = s_sp

        TE = Eng("tensor", s_mm)
        VE = Eng("vector", s_ve)
        SE = Eng("scalar", s_sc)
        GE = Eng("gpsimd", s_cc, dinc=1)
        SY = Eng("sync", s_in, dinc=16)
        ALL_ENG = (TE, VE, SE, GE, SY)

        def ph(label):
            PHASES.append((label, {e.name: len(e.q) for e in ALL_ENG}))

        MEM_W = {}
        MEM_R = {}
        ASYNC = object()

        def _span(ap):
            if isinstance(ap, tuple):
                return ap
            off = ap.offset
            tot = 0
            for st, ct in ap.ap:
                tot += (ct - 1) * abs(st)
            return ap.tensor.name, off, off + tot

        def E(eng, fn, rd=(), wr=(), gid=None, inc=None, sem=None,
              asyn=False):
            rds = [_span(a) for a in rd]
            wrs = [_span(a) for a in wr]
            def edge(sem, mark):
                eng.wait(sem, mark)

            for nm, lo, hi in rds:
                for w in MEM_W.get(nm, ()):
                    if w[0] <= hi and lo <= w[1]:
                        if gid is not None and w[2] is eng and w[5] == gid:
                            continue
                        edge(w[3], w[4])
            for nm, lo, hi in wrs:
                for w in MEM_W.get(nm, ()):
                    if w[0] <= hi and lo <= w[1]:
                        if gid is not None and w[2] is eng and w[5] == gid:
                            continue
                        edge(w[3], w[4])
                for r in MEM_R.get(nm, ()):
                    if r[0] <= hi and lo <= r[1] and (asyn
                                                      or r[2] is not eng):
                        edge(r[3], r[4])
            mark = eng.op(fn, inc=inc, sem=sem)
            msem = eng.sem if sem is None else sem
            reng = ASYNC if asyn else eng
            for nm, lo, hi in rds:
                rec = [lo, hi, reng, msem, mark]
                MEM_R.setdefault(nm, []).append(rec)
                if asyn and BATCH[0] is not None:
                    BATCH[0].append(rec)
            for nm, lo, hi in wrs:
                lw = MEM_W.setdefault(nm, [])
                lw[:] = [w for w in lw if not (lo <= w[0] and w[1] <= hi)]
                rec = [lo, hi, reng, msem, mark, gid]
                lw.append(rec)
                if asyn and BATCH[0] is not None:
                    BATCH[0].append(rec)
                lr = MEM_R.get(nm)
                if lr:
                    lr[:] = [r for r in lr if not (lo <= r[0] and r[1] <= hi)]
            return mark

        open_groups = []
        gid_ctr = [0]

        def _psum_row_span(sp):
            nm, lo, hi = sp
            return (nm, (lo // 512) * 512, (hi // 512) * 512 + 511)

        def t_mm(o, lh, rh, start=True, stop=True, tp=None):
            sp = _span(o)
            row = _psum_row_span(sp)
            if start:
                gid_ctr[0] += 1
                gid = gid_ctr[0]
                open_groups[:] = [g for g in open_groups
                                  if not (g[0][0] == row[0]
                                          and g[0][1] <= row[2]
                                          and row[1] <= g[0][2])]
                open_groups.append((row, gid))
                wsp = row
            else:
                gid = None
                for (gi, gg) in open_groups:
                    if gi[0] == sp[0] and gi[1] <= sp[2] and sp[1] <= gi[2]:
                        gid = gg
                        break
                wsp = sp
            E(TE,
              lambda e, oo=o, ll=lh, rr=rh, s0=start, s1=stop, t0=tp:
              e.matmul(oo, ll, rr, start=s0, stop=s1, tile_position=t0,
                       skip_group_check=True),
              rd=[lh, rh], wr=[wsp], gid=gid)

        def mm_group(jobs):
            firsts, lasts = {}, {}
            for i, j in enumerate(jobs):
                k = j[4]
                firsts.setdefault(k, i)
                lasts[k] = i
            for i, (o, lh, rh, tp, k) in enumerate(jobs):
                t_mm(o, lh, rh, start=(firsts[k] == i), stop=(lasts[k] == i),
                     tp=tp)

        def v_tt(o, a, b, op):
            E(VE, lambda e, oo=o, aa=a, bb=b, pp=op:
              e.tensor_tensor(oo, aa, bb, pp), rd=[a, b], wr=[o])

        def v_ts(o, a, s1, s2, op0, op1=AL.bypass):
            rd = [a]
            if hasattr(s1, "tensor"):
                rd.append(s1)
            if hasattr(s2, "tensor"):
                rd.append(s2)
            E(VE, lambda e, oo=o, aa=a, x1=s1, x2=s2, p0=op0, p1=op1:
              e.tensor_scalar(oo, aa, x1, x2, p0, p1), rd=rd, wr=[o])

        def v_stt(o, a, s, b, op0, op1):
            rd = [a, b]
            if hasattr(s, "tensor"):
                rd.append(s)
            E(VE, lambda e, oo=o, aa=a, ss=s, bb=b, p0=op0, p1=op1:
              e.scalar_tensor_tensor(oo, aa, ss, bb, p0, p1), rd=rd, wr=[o])

        def v_cp(o, a):
            E(VE, lambda e, oo=o, aa=a: e.tensor_copy(oo, aa),
              rd=[a], wr=[o])

        def v_red(o, a, op, absv=False):
            E(VE, lambda e, oo=o, aa=a, pp=op, av=absv:
              e.tensor_reduce(oo, aa, AXX, pp, apply_absolute_value=av),
              rd=[a], wr=[o])

        def v_recip(o, a):
            E(VE, lambda e, oo=o, aa=a: e.reciprocal(oo, aa),
              rd=[a], wr=[o])

        def v_set(o, val):
            E(VE, lambda e, oo=o, vv=val: e.memset(oo, vv), wr=[o])

        def s_act(o, a, fn, bias=0.0, scale=1.0):
            rd = [a]
            if hasattr(bias, "tensor"):
                rd.append(bias)
            E(SE, lambda e, oo=o, aa=a, ff=fn, bb=bias, ss=scale:
              e.activation(oo, aa, ff, bias=bb, scale=ss), rd=rd, wr=[o])

        def s_cp(o, a):
            s_act(o, a, ACT.Copy)

        def fm(t, q, c0, w, W=TS):
            return t[:, q * W + c0: q * W + c0 + w]

        def fm1(t, q, c0, w):
            return t[:, q * TP1 + c0: q * TP1 + c0 + w]

        BATCH = [None]

        def dma_begin():
            SY.wait(s_in, s_in.v)
            SE.wait(s_sp, s_sp.v)
            BATCH[0] = []

        def dma_end():
            for rec in BATCH[0]:
                rec[4] = rec[3].v
            BATCH[0] = None

        def dma_in(dst, src):
            if BATCH[0] is None:
                SY.wait(s_in, s_in.v)
            return E(SY, lambda e, dd=dst, ss=src: e.dma_start(dd, ss),
                     rd=[src], wr=[dst], asyn=True)

        def dma_sp(dst, src, slow=False):
            if BATCH[0] is None:
                SE.wait(s_sp, s_sp.v)
            return E(SE,
                     lambda e, dd=dst, ss=src, sl=slow:
                     e.dma_start(dd, ss, allow_slow_non_contiguous=sl)
                     if sl else e.dma_start(dd, ss),
                     rd=[src], wr=[dst], sem=s_sp, inc=16, asyn=True)

        for nm in PNAMES:
            dma_in(par[nm][:, :], p_h[nm].ap())
        dma_in(idn32[:, :], idn_h.ap())
        dma_in(msu[:, :], msu_h.ap())
        dma_in(msl[:, :], msl_h.ap())
        dma_in(cval[:, :], cval_h.ap())
        in_consts = dma_in(sel128[0:1, 0:8], sel_h.ap())

        SE.wait(s_in, in_consts)
        s_cp(idn[:, :], idn32[:, :])
        s_cp(srowb[0:1, 0:8], sel128[0:1, 0:8])
        sc_idn = SE.chk(s_sc)

        v_set(ones_r[:, :], 1.0)
        v_set(ones_c[:, :], 1.0)
        v_set(ones_r32[:, :], 1.0)
        v_set(gcol[:, :], 0.0)
        v_set(gmax[:, :], 0.0)
        v_set(ldg[:, :], 0.0)
        v_set(z32[:, :], 0.0)
        ve_consts = VE.chk(s_ve)

        wbufs = (wstg, wstg2)
        wb_i = [0]

        def stage(dst, src_ap, rows, cols):
            wb = wbufs[wb_i[0] & 1]
            wb_i[0] += 1
            dma_in(wb[0:rows, 0:cols], src_ap)
            s_cp(dst, wb[0:rows, 0:cols])

        for q in range(PCH):
            stage(w_m1[:, q * 192:(q + 1) * 192],
                  maa1_h.ap()[q * 128:(q + 1) * 128, :], 128, 192)
        for s in range(6):
            rb, cb = 32 * (s % 3), (s // 3) * C
            stage(w_m2[rb:rb + 32, cb:cb + C], maa2_h.ap()[s, :, :], 32, C)

        def big_weight(dst, hbm):
            for q in range(PCH):
                stage(dst[:, q * C:(q + 1) * C],
                      hbm.ap()[q * 128:(q + 1) * 128, :], 128, C)

        def weights_gen():
            for q in range(PCH):
                stage(w_d1[:, q * 64:(q + 1) * 64],
                      d1_h.ap()[q * 128:(q + 1) * 128, :], 128, 64)
                stage(w_a1[:, q * 64:(q + 1) * 64],
                      a1_h.ap()[q * 128:(q + 1) * 128, :], 128, 64)
                yield
                stage(w_k1[:, q * 64:(q + 1) * 64],
                      k1_h.ap()[q * 128:(q + 1) * 128, :], 128, 64)
                stage(w_g1[:, q * 128:(q + 1) * 128],
                      g1_h.ap()[q * 128:(q + 1) * 128, :], 128, 128)
                yield
            stage(w_d2[:, :], d2_h.ap(), 64, C)
            stage(w_a2[:, :], a2_h.ap(), 64, C)
            yield
            stage(w_k2[:, :], k2_h.ap(), 64, C)
            stage(w_g2[:, :], g2_h.ap(), 128, C)
            yield
            stage(hind[:, :], hind_h.ap(), 128, PCH * 16)
            stage(hexp[0:16, :], hexp_h.ap(), 16, PCH * 128)
            yield
            for wdst, wsrc in ((w_k, W_h["W_k"]), (w_v, W_h["W_v"]),
                               (w_r, W_h["W_r"])):
                for q in range(PCH):
                    stage(wdst[:, q * C:(q + 1) * C],
                          wsrc.ap()[q * 128:(q + 1) * 128, :], 128, C)
                    yield

        TE.wait(s_sc, sc_idn)
        TE.wait(s_ve, ve_consts)
        t_mm(psA[:, 0:8], ones_r[0:1, :], srowb[0:1, 0:8])
        mm_sel = TE.chk(s_mm)
        VE.wait(s_mm, mm_sel)
        v_cp(sel128[:, 0:8], psA[:, 0:8])
        VE.chk(s_ve)

        SE.wait(s_ve, ve_consts)
        for p in range(NP8):
            s_cp(z32[EV, p * 128:p * 128 + 64], idn32[EV, 0:64])
            s_cp(z32[OD, p * 128:p * 128 + 64], idn32[OD, 64:128])
        sc_z = SE.chk(s_sc)
        VE.wait(s_sc, sc_z)
        v_cp(zbf[:, :], z32[:, :])
        VE.chk(s_ve)
        init_done_sc = s_sc.v
        init_done_ve = s_ve.v

        ph("p0_weights")

        v_tB = (v_t, v_t2)
        b_tB = (b_t, b_t2)
        kq_tB = (kq_t, kq_t2)
        r_tB = (r_t, r_t2)
        kscB = (ksc, ksc2)

        def prep_tile(ti):
            tok0 = ti * TS
            pa = ti % 2
            Pv, Pb, Pk, Pr, Pks = (v_tB[pa], b_tB[pa], kq_tB[pa], r_tB[pa],
                                   kscB[pa])
            ph("tile_t%d" % ti)

            for q in range(PCH):
                dma_in(fm1(x32, q, 0, TP1),
                       xT_h.ap()[q * 128:(q + 1) * 128, tok0:tok0 + TP1])
            s_cp(xbf[:, :], x32[:, :])
            s_act(x2bf[:, :], x32[:, :], ACT.Square)
            yield

            for q in range(PCH):
                t_mm(psE[0:1, 0:TP1], ones_c[:, 0:1], fm1(xbf, q, 0, TP1),
                     start=(q == 0), stop=(q == PCH - 1))
            for q in range(PCH):
                t_mm(psF[0:1, 0:TP1], ones_c[:, 0:1], fm1(x2bf, q, 0, TP1),
                     start=(q == 0), stop=(q == PCH - 1))
            v_ts(srow[0:1, 0:TP1], psE[0:1, 0:TP1], 1.0 / C, None, AL.mult)
            v_ts(srow[0:1, TP1:2 * TP1], psF[0:1, 0:TP1], 1.0 / C, None,
                 AL.mult)
            s_act(srow2[0:1, 0:TP1], srow[0:1, 0:TP1], ACT.Square)
            v_tt(srow2[0:1, TP1:2 * TP1], srow[0:1, TP1:2 * TP1],
                 srow2[0:1, 0:TP1], AL.subtract)
            s_act(srow2[0:1, 2 * TP1:3 * TP1], srow2[0:1, TP1:2 * TP1],
                  ACT.Sqrt, bias=LN1_EPS)
            v_recip(srow[0:1, 2 * TP1:3 * TP1], srow2[0:1, 2 * TP1:3 * TP1])
            v_tt(srow2[0:1, 0:TP1], srow[0:1, 0:TP1],
                 srow[0:1, 2 * TP1:3 * TP1], AL.mult)
            v_ts(srow2[0:1, 0:TP1], srow2[0:1, 0:TP1], -1.0, None, AL.mult)
            v_cp(srowb[0:1, 0:TP1], srow[0:1, 2 * TP1:3 * TP1])
            v_cp(srowb[0:1, TP1:2 * TP1], srow2[0:1, 0:TP1])
            yield

            t_mm(psE[:, 0:256], ones_r[0:1, :], srowb[0:1, 0:256])
            t_mm(psE[:, 256:257], ones_r[0:1, :], srowb[0:1, 256:257])
            t_mm(psF[:, 0:256], ones_r[0:1, :], srowb[0:1, TP1:TP1 + 256])
            t_mm(psF[:, 256:257], ones_r[0:1, :],
                 srowb[0:1, TP1 + 256:TP1 + 257])
            s_cp(bcA[:, 0:TP1], psE[:, 0:TP1])
            s_cp(bcB[:, 0:TP1], psF[:, 0:TP1])
            for q in range(PCH):
                v_tt(fm1(xln, q, 0, TP1), fm1(xbf, q, 0, TP1), bcA[:, 0:TP1],
                     AL.mult)
            for q in range(PCH):
                v_tt(fm1(xln, q, 0, TP1), fm1(xln, q, 0, TP1), bcB[:, 0:TP1],
                     AL.add)
            for q in range(PCH):
                v_tt(fm(xx, q, 0, TS), fm1(xln, q, 0, TS), fm1(xln, q, 1, TS),
                     AL.subtract)
            yield

            for q in range(PCH):
                v_stt(fm(xs, q, 0, TS), fm(xx, q, 0, TS),
                      par["maa_x"][:, q:q + 1], fm1(xln, q, 1, TS),
                      AL.mult, AL.add)
            for q in range(PCH):
                t_mm(psE[0:96, 0:TS], w_m1[:, q * 192:q * 192 + 96],
                     fm(xs, q, 0, TS), start=(q == 0), stop=(q == PCH - 1))
            for q in range(PCH):
                t_mm(psF[0:96, 0:TS], w_m1[:, q * 192 + 96:(q + 1) * 192],
                     fm(xs, q, 0, TS), start=(q == 0), stop=(q == PCH - 1))
            s_act(mixs[0:96, 0:TS], psE[0:96, 0:TS], ACT.Tanh)
            s_act(mixs[0:96, TS:2 * TS], psF[0:96, 0:TS], ACT.Tanh)
            yield

            def branch_lerp(sname, sidx, xsb):
                rb = 32 * (sidx % 3)
                cb = (sidx // 3) * TS
                for mp in range(0, PCH, 2):
                    ps = psE if (mp % 4 == 0) else psF
                    for m2 in (mp, mp + 1):
                        co = (m2 & 1) * TS
                        t_mm(ps[:, co:co + TS],
                             w_m2[rb:rb + 32,
                                  (sidx // 3) * C + m2 * 128:
                                  (sidx // 3) * C + (m2 + 1) * 128],
                             mixs[rb:rb + 32, cb:cb + TS],
                             start=(co == 0), stop=(co == TS), tp=(rb, 0))
                    s_cp(dsc[:, mp * TS:(mp + 2) * TS], ps[:, 0:2 * TS])
                for q in range(PCH):
                    v_stt(fm(xsb, q, 0, TS), fm(xx, q, 0, TS),
                          par["maa_" + sname][:, q:q + 1], fm1(xln, q, 1, TS),
                          AL.mult, AL.add)
                v_tt(dsc[:, 0:FT], dsc[:, 0:FT], xx[:, 0:FT], AL.mult)
                v_tt(xsb[:, 0:FT], xsb[:, 0:FT], dsc[:, 0:FT], AL.add)

            def big_proj(wt, evac, xsb):
                for mp in range(0, PCH, 2):
                    ps = psE if (mp % 4 == 0) else psF
                    for m2 in (mp, mp + 1):
                        co = (m2 & 1) * TS
                        for q in range(PCH):
                            t_mm(ps[:, co:co + TS],
                                 wt[:, q * C + m2 * 128:
                                    q * C + (m2 + 1) * 128],
                                 fm(xsb, q, 0, TS),
                                 start=(q == 0 and co == 0),
                                 stop=(q == PCH - 1 and co == TS))
                    evac(mp, ps)

            branch_lerp("w", 1, xs)
            for q in range(PCH):
                t_mm(psE[0:64, 0:TS], w_d1[:, q * 64:(q + 1) * 64],
                     fm(xs, q, 0, TS), start=(q == 0), stop=(q == PCH - 1))
            s_act(hid64[0:64, 0:TS], psE[0:64, 0:TS], ACT.Tanh)
            yield
            for m in range(PCH):
                t_mm(psF[:, 0:TS], w_d2[0:64, m * 128:(m + 1) * 128],
                     hid64[0:64, 0:TS], start=True, stop=True)
                s_act(fm(lg_t, m, 0, TS), psF[:, 0:TS], ACT.Exp,
                      bias=par["ndecay"][:, m:m + 1], scale=-1.0)
            s_act(lg_t[:, :], lg_t[:, :], ACT.Ln, bias=1.0)
            s_act(Pks[:, 0:FT], lg_t[:, :], ACT.Exp, bias=-0.25, scale=-0.5)
            s_act(lg_t[:, :], lg_t[:, :], ACT.Exp, bias=-0.5, scale=-1.0)
            yield

            v_set(bass.AP(ldg, 0, [[TP1 * PCH, 128], [TP1, PCH]]), 0.0)
            for q in range(PCH):
                E(VE, lambda e, qq=q: e.tensor_tensor_scan(
                    fm1(ldg, qq, 1, TS),
                    bass.AP(ones_c, 0, [[1, 128], [0, TS]]),
                    fm(lg_t, qq, 0, TS), 0.0, AL.mult, AL.subtract),
                  rd=[bass.AP(ones_c, 0, [[1, 128], [0, TS]]),
                      fm(lg_t, q, 0, TS), fm1(ldg, q, 0, 1)],
                  wr=[fm1(ldg, q, 1, TS)])
            v_cp(bases[:, :],
                 bass.AP(ldg, 0, [[TP1 * PCH, 128], [TP1, PCH], [64, CPT]]))
            for q in range(PCH):
                v_tt(fm1(ldrel, q, 0, TS), fm1(ldg, q, 1, TS),
                     bass.AP(bases, q * CPT,
                             [[PCH * CPT, 128], [1, CPT], [0, 64]]),
                     AL.subtract)
            yield
            s_act(bass.AP(d63, ti * CPT,
                          [[PCH * NCH, 128], [NCH, PCH], [1, CPT]]),
                  bass.AP(ldrel, 63, [[TP1 * PCH, 128], [TP1, PCH], [64, CPT]]),
                  ACT.Exp)
            for q in range(PCH):
                s_act(fm1(epg, q, 1, TS), fm1(ldrel, q, 0, TS), ACT.Exp)
                s_act(fm1(en_t, q, 1, TS), fm1(ldrel, q, 0, TS), ACT.Exp,
                      scale=-1.0)
            v_set(bass.AP(epg, 0, [[TP1 * PCH, 128], [TP1, PCH]]), 1.0)
            yield

            branch_lerp("k", 2, xs2)
            for q in range(PCH):
                t_mm(psE[0:64, 0:TS], w_k1[:, q * 64:(q + 1) * 64],
                     fm(xs2, q, 0, TS), start=(q == 0), stop=(q == PCH - 1))
            s_act(hid64[0:64, 0:TS], psE[0:64, 0:TS], ACT.Tanh)
            yield

            def evac_k(m, ps):
                v_tt(Pk[:, m * TS:(m + 2) * TS], ps[:, 0:2 * TS],
                     Pks[:, m * TS:(m + 2) * TS], AL.mult)
                pso = psF if ps is psE else psE
                t_mm(pso[:, 0:TS], w_k2[0:64, m * 128:(m + 1) * 128],
                     hid64[0:64, 0:TS], start=True, stop=True)
                t_mm(pso[:, TS:2 * TS],
                     w_k2[0:64, (m + 1) * 128:(m + 2) * 128],
                     hid64[0:64, 0:TS], start=True, stop=True)
                s_cp(hid128[:, 0:TS], pso[:, 0:TS])
                v_tt(fm(kk_t, m, 0, TS), ps[:, 0:TS], hid128[:, 0:TS], AL.add)
                s_cp(hid128[:, 0:TS], pso[:, TS:2 * TS])
                v_tt(fm(kk_t, m + 1, 0, TS), ps[:, TS:2 * TS],
                     hid128[:, 0:TS], AL.add)

            big_proj(w_k, evac_k, xs2)
            yield

            for q in range(PCH):
                s_act(fm(sqb, q, 0, TS), fm(kk_t, q, 0, TS), ACT.Square)
            for q in range(PCH):
                t_mm(psE[0:1, 0:TS], ones_c[:, 0:1], fm(sqb, q, 0, TS),
                     start=(q == 0), stop=(q == PCH - 1))
            s_act(srow[0:1, 0:TS], psE[0:1, 0:TS], ACT.Sqrt)
            v_ts(srow[0:1, 0:TS], srow[0:1, 0:TS], 1e-12, None, AL.max)
            v_recip(srow[0:1, TS:2 * TS], srow[0:1, 0:TS])
            v_cp(srowb[0:1, 0:TS], srow[0:1, TS:2 * TS])
            t_mm(psF[:, 0:TS], ones_r[0:1, :], srowb[0:1, 0:TS])
            for q in range(PCH):
                v_tt(fm(kk_t, q, 0, TS), fm(kk_t, q, 0, TS), psF[:, 0:TS],
                     AL.mult)
            yield

            branch_lerp("a", 4, xs)
            for q in range(PCH):
                t_mm(psE[0:64, 0:TS], w_a1[:, q * 64:(q + 1) * 64],
                     fm(xs, q, 0, TS), start=(q == 0), stop=(q == PCH - 1))
            s_cp(hid64[0:64, 0:TS], psE[0:64, 0:TS])
            yield
            for m in range(PCH):
                t_mm(psF[:, 0:TS], w_a2[0:64, m * 128:(m + 1) * 128],
                     hid64[0:64, 0:TS], start=True, stop=True)
                s_act(fm(dsc, m, 0, TS), psF[:, 0:TS], ACT.Sigmoid,
                      bias=par["aaaaa"][:, m:m + 1])
            for q in range(PCH):
                v_stt(fm(Pb, q, 0, TS), fm(dsc, q, 0, TS), 2.0,
                      fm(kk_t, q, 0, TS), AL.mult, AL.mult)
            yield

            branch_lerp("g", 5, xs2)
            for q in range(PCH):
                t_mm(psE[0:128, 0:TS], w_g1[:, q * 128:(q + 1) * 128],
                     fm(xs2, q, 0, TS), start=(q == 0), stop=(q == PCH - 1))
            s_act(hid128[:, 0:TS], psE[:, 0:TS], ACT.Tanh)
            yield
            for mp in range(0, PCH, 2):
                for m2 in (mp, mp + 1):
                    co = (m2 & 1) * TS
                    t_mm(psF[:, co:co + TS], w_g2[:, m2 * 128:(m2 + 1) * 128],
                         hid128[:, 0:TS], start=(co == 0), stop=(co == TS))
                s_act(bass.AP(g_t, mp * TP1, [[FT1, 128], [TP1, 2], [1, TS]]),
                      psF[:, 0:2 * TS], ACT.Copy)
            dma_sp(bass.AP(gsp_h, tok0, [[TC, 128], [128 * TC, PCH], [1, TS]]),
                   bass.AP(g_t, 0, [[FT1, 128], [TP1, PCH], [1, TS]]))
            yield

            branch_lerp("v", 3, xs)

            def evac_v(m, ps):
                s_act(Pv[:, m * TS:(m + 2) * TS], ps[:, 0:2 * TS], ACT.Copy)

            big_proj(w_v, evac_v, xs)
            dma_sp(bass.AP(vsp_h, tok0, [[TC, 128], [128 * TC, PCH], [1, TS]]),
                   bass.AP(Pv, 0, [[FT, 128], [TS, PCH], [1, TS]]))
            yield

            branch_lerp("r", 0, xs2)

            def evac_r(m, ps):
                s_act(Pr[:, m * TS:(m + 2) * TS], ps[:, 0:2 * TS], ACT.Copy)

            big_proj(w_r, evac_r, xs2)
            yield

            for q in range(PCH):
                v_stt(fm(xs, q, 0, TS), fm(Pr, q, 0, TS),
                      par["faaaa"][:, q:q + 1], fm(Pk, q, 0, TS),
                      AL.mult, AL.mult)
            for q in range(PCH):
                t_mm(psE[0:16, 0:TS], hind[:, q * 16:(q + 1) * 16],
                     fm(xs, q, 0, TS), start=(q == 0), stop=(q == PCH - 1))
            s_cp(u_bf[0:16, tok0:tok0 + TS], psE[0:16, 0:TS])
            yield

            for q in range(PCH):
                v_stt(fm(Pks, q, 0, TS), fm(kk_t, q, 0, TS), -1.0,
                      fm1(epg, q, 0, TS), AL.mult, AL.mult)
            v_ts(bass.AP(Pks, 64, [[FT, 128], [TS, PCH], [64, CPT - 1]]),
                 bass.AP(kk_t, 64, [[FT, 128], [TS, PCH], [64, CPT - 1]]),
                 -1.0, None, AL.mult)
            for q in range(PCH):
                v_tt(fm(Pb, q, 0, TS), fm(Pb, q, 0, TS), fm1(en_t, q, 1, TS),
                     AL.mult)
            for q in range(PCH):
                v_tt(fm(Pk, q, 0, TS), fm(Pk, q, 0, TS),
                     fm1(en_t, q, 1, TS), AL.mult)
            for q in range(PCH):
                v_tt(fm(Pr, q, 0, TS), fm(Pr, q, 0, TS), fm1(epg, q, 1, TS),
                     AL.mult)

            if DEBUG:
                GE.wait(s_ve, s_ve.v)
                for nm, tt in (("d_r", Pr), ("d_kq", Pk), ("d_kk", kk_t),
                               ("d_b", Pb), ("d_af", Pks)):
                    dma_sp(bass.AP(dbg_h[nm], tok0,
                                   [[TC, 128], [128 * TC, PCH], [1, TS]]),
                           bass.AP(tt, 0, [[FT, 128], [TS, PCH], [1, TS]]))
                dma_sp(bass.AP(dbg_h["d_xln"], tok0,
                               [[TC, 128], [128 * TC, PCH], [1, TS]]),
                       bass.AP(xln, 1, [[FT1, 128], [TP1, PCH], [1, TS]]))
                dma_sp(bass.AP(dbg_h["d_ld"], tok0,
                               [[TC, 128], [128 * TC, PCH], [1, TS]]),
                       bass.AP(ldrel, 0, [[FT1, 128], [TP1, PCH], [1, TS]]))
                dma_sp(bass.AP(dbg_h["d_eng"], tok0,
                               [[TC, 128], [128 * TC, PCH], [1, TS]]),
                       bass.AP(en_t, 1, [[FT1, 128], [TP1, PCH], [1, TS]]))
            yield

        def scan_tile(ti):
            tok0 = ti * TS
            pa = ti % 2
            Pv, Pb, Pk, Pr, Pks = (v_tB[pa], b_tB[pa], kq_tB[pa], r_tB[pa],
                                   kscB[pa])
            ph("sweep1_t%d" % ti)
            for cc in range(CPT):
                c = ti * CPT + cc
                t0 = cc * L

                jobs = []
                for p in range(NP8):
                    for src, ps, key in ((Pv, psB, "tv"), (Pb, psC, "tb"),
                                         (Pk, psD, "tk")):
                        sl = fm(src, p, t0, L)
                        jobs.append((ps[EV, p * 64:p * 64 + 64], sl[EV, :],
                                     idn[EV, 0:64], TPE, (key, 0)))
                        jobs.append((ps[OD, p * 64:p * 64 + 64], sl[OD, :],
                                     idn[OD, 64:128], TPO, (key, 1)))
                mm_group(jobs)
                s_cp(vtm[:, :], psB[:, 0:512])
                s_cp(btm[:, :], psC[:, 0:512])
                s_cp(ktm[:, :], psD[:, 0:512])
                yield

                jobs = []
                for p in range(NP8):
                    afs = fm(Pks, p, t0, L)
                    bfs = fm(Pb, p, t0, L)
                    jobs.append((psA[EV, p * 64:p * 64 + 64], afs[EV, :],
                                 bfs[EV, :], TPE, ("A", 0)))
                    jobs.append((psA[OD, p * 64:p * 64 + 64], afs[OD, :],
                                 bfs[OD, :], TPO, ("A", 1)))
                mm_group(jobs)
                yield
                jobs = []
                for p in range(NP8):
                    ps, pc = (psB, p * 128) if p < 4 else (psC, (p - 4) * 128)
                    afs = fm(Pks, p, t0, L)
                    jobs.append((ps[EV, pc:pc + 128], afs[EV, :],
                                 zbf[EV, p * 128:(p + 1) * 128], TPE,
                                 ("R" + str(p // 4), 0)))
                    jobs.append((ps[OD, pc:pc + 128], afs[OD, :],
                                 zbf[OD, p * 128:(p + 1) * 128], TPO,
                                 ("R" + str(p // 4), 1)))
                firsts = {}
                for i, j in enumerate(jobs):
                    firsts.setdefault(j[4], i)
                for i, (o, lh, rh, tp, kkey) in enumerate(jobs):
                    t_mm(o, lh, rh, start=(firsts[kkey] == i), stop=False,
                         tp=tp)
                v_tt(m_sl[:, :], psA[:, 0:512],
                     bass.AP(msl, 0, [[64, 128], [0, NP8], [1, 64]]), AL.mult)
                yield

                jobs = []
                for p in range(NP8):
                    ps, pc = (psD, p * 128) if p < 4 else (psA, (p - 4) * 128)
                    bfs = fm(Pb, p, t0, L)
                    afs = fm(Pks, p, t0, L)
                    rfs = fm(Pr, p, t0, L)
                    jobs.append((ps[EV, pc:pc + 64], bfs[EV, :], afs[EV, :],
                                 TPE, ("D" + str(p // 4), 0)))
                    jobs.append((ps[OD, pc:pc + 64], bfs[OD, :], afs[OD, :],
                                 TPO, ("D" + str(p // 4), 1)))
                    jobs.append((ps[EV, pc + 64:pc + 128], bfs[EV, :],
                                 rfs[EV, :], TPE, ("D" + str(p // 4), 0)))
                    jobs.append((ps[OD, pc + 64:pc + 128], bfs[OD, :],
                                 rfs[OD, :], TPO, ("D" + str(p // 4), 1)))
                mm_group(jobs)
                for half, ps in ((0, psD), (1, psA)):
                    v_tt(m_su[:, half * 256:half * 256 + 256],
                         bass.AP(ps, 0, [[512, 128], [128, 4], [1, 64]]),
                         bass.AP(msu, 0, [[128, 128], [0, 4], [1, 64]]),
                         AL.mult)
                    v_tt(lrb[:, half * 256:half * 256 + 256],
                         bass.AP(ps, 64, [[512, 128], [128, 4], [1, 64]]),
                         bass.AP(msu, 64, [[128, 128], [0, 4], [1, 64]]),
                         AL.mult)
                yield

                jobs = []
                for p in range(NP8):
                    ps, pc = (psG, p * 128) if p < 4 else (psH, (p - 4) * 128)
                    kfs = fm(Pk, p, t0, L)
                    afs = fm(Pks, p, t0, L)
                    rfs = fm(Pr, p, t0, L)
                    jobs.append((ps[EV, pc:pc + 64], kfs[EV, :], afs[EV, :],
                                 TPE, ("D" + str(p // 4), 0)))
                    jobs.append((ps[OD, pc:pc + 64], kfs[OD, :], afs[OD, :],
                                 TPO, ("D" + str(p // 4), 1)))
                    jobs.append((ps[EV, pc + 64:pc + 128], kfs[EV, :],
                                 rfs[EV, :], TPE, ("D" + str(p // 4), 0)))
                    jobs.append((ps[OD, pc + 64:pc + 128], kfs[OD, :],
                                 rfs[OD, :], TPO, ("D" + str(p // 4), 1)))
                mm_group(jobs)
                for half, ps in ((0, psG), (1, psH)):
                    v_tt(t1l[:, half * 256:half * 256 + 256],
                         bass.AP(ps, 0, [[512, 128], [128, 4], [1, 64]]),
                         bass.AP(msu, 0, [[128, 128], [0, 4], [1, 64]]),
                         AL.mult)
                    v_tt(lrk[:, half * 256:half * 256 + 256],
                         bass.AP(ps, 64, [[512, 128], [128, 4], [1, 64]]),
                         bass.AP(msu, 64, [[128, 128], [0, 4], [1, 64]]),
                         AL.mult)
                yield

                jobs = []
                for p in range(NP8):
                    ps, pc = (psB, p * 128) if p < 4 else (psC, (p - 4) * 128)
                    jobs.append((ps[EV, pc + 64:pc + 128],
                                 t1l[EV, p * 64:p * 64 + 64],
                                 vtm[EV, p * 64:p * 64 + 64], TPE,
                                 ("T" + str(p // 4), 0)))
                    jobs.append((ps[OD, pc + 64:pc + 128],
                                 t1l[OD, p * 64:p * 64 + 64],
                                 vtm[OD, p * 64:p * 64 + 64], TPO,
                                 ("T" + str(p // 4), 1)))
                lastk = {}
                for i, j in enumerate(jobs):
                    lastk[j[4]] = i
                for i, (o, lh, rh, tp, kkey) in enumerate(jobs):
                    t_mm(o, lh, rh, start=False, stop=(lastk[kkey] == i), tp=tp)
                s_cp(xa_t[:, 0:512], psB[:, 0:512])
                s_cp(xa_t[:, 512:1024], psC[:, 0:512])
                yield

                xcur, xnxt = xa_t, xb_t
                pcur, ncur, pnxt, nnxt = pj_a, nj_a, pj_b, nj_b
                for j in range(NSOLVE):
                    pref = m_sl if j == 0 else pcur
                    nref = m_su if j == 0 else ncur
                    for h in (0, 1):
                        ps = psB if h == 0 else psC
                        jobs = []
                        for p in range(h * 4, h * 4 + 4):
                            pc = (p - h * 4) * 128
                            xsl = xcur[:, p * 128:(p + 1) * 128]
                            jobs.append((ps[EV, pc:pc + 128],
                                         nref[EV, p * 64:p * 64 + 64],
                                         xsl[EV, :], TPE, ("X", h, 0)))
                            jobs.append((ps[OD, pc:pc + 128],
                                         nref[OD, p * 64:p * 64 + 64],
                                         xsl[OD, :], TPO, ("X", h, 1)))
                        if j < NSOLVE - 1:
                            for p in range(h * 4, h * 4 + 4):
                                co = p * 64
                                jobs.append((psD[EV, co:co + 64],
                                             nref[EV, co:co + 64],
                                             pref[EV, co:co + 64], TPE,
                                             ("P", h, 0)))
                                jobs.append((psD[OD, co:co + 64],
                                             nref[OD, co:co + 64],
                                             pref[OD, co:co + 64], TPO,
                                             ("P", h, 1)))
                                jobs.append((psA[EV, co:co + 64],
                                             pref[EV, co:co + 64],
                                             nref[EV, co:co + 64], TPE,
                                             ("N", h, 0)))
                                jobs.append((psA[OD, co:co + 64],
                                             pref[OD, co:co + 64],
                                             nref[OD, co:co + 64], TPO,
                                             ("N", h, 1)))
                        mm_group(jobs)
                        v_tt(xnxt[:, h * 512:(h + 1) * 512], ps[:, 0:512],
                             xcur[:, h * 512:(h + 1) * 512], AL.add)
                        if j < NSOLVE - 1:
                            s_cp(pnxt[:, h * 256:(h + 1) * 256],
                                 psD[:, h * 256:(h + 1) * 256])
                            s_cp(nnxt[:, h * 256:(h + 1) * 256],
                                 psA[:, h * 256:(h + 1) * 256])
                    if j < NSOLVE - 1:
                        pcur, pnxt = pnxt, pcur
                        ncur, nnxt = nnxt, ncur
                    xcur, xnxt = xnxt, xcur
                    if j % 2 == 1:
                        yield
                xfin = xcur

                for h in (0, 1):
                    jobs = []
                    for p in range(h * 4, h * 4 + 4):
                        zD = zbf[:, p * 128 + 64:(p + 1) * 128]
                        zP = zbf[:, p * 128:p * 128 + 64]
                        rfs = fm(Pr, p, t0, L)
                        co = p * 64
                        jobs.append((psD[EV, co:co + 64], zD[EV, :],
                                     rfs[EV, :], TPE, ("Y", h, 0)))
                        jobs.append((psD[OD, co:co + 64], zD[OD, :],
                                     rfs[OD, :], TPO, ("Y", h, 1)))
                        jobs.append((psA[EV, co:co + 64], zP[EV, :],
                                     rfs[EV, :], TPE, ("C", h, 0)))
                        jobs.append((psA[OD, co:co + 64], zP[OD, :],
                                     rfs[OD, :], TPO, ("C", h, 1)))
                    for p in range(h * 4, h * 4 + 4):
                        xD = xfin[:, p * 128 + 64:(p + 1) * 128]
                        xP = xfin[:, p * 128:p * 128 + 64]
                        co = p * 64
                        jobs.append((psD[EV, co:co + 64], xD[EV, :],
                                     lrb[EV, co:co + 64], TPE, ("Y", h, 0)))
                        jobs.append((psD[OD, co:co + 64], xD[OD, :],
                                     lrb[OD, co:co + 64], TPO, ("Y", h, 1)))
                        jobs.append((psA[EV, co:co + 64], xP[EV, :],
                                     lrb[EV, co:co + 64], TPE, ("C", h, 0)))
                        jobs.append((psA[OD, co:co + 64], xP[OD, :],
                                     lrb[OD, co:co + 64], TPO, ("C", h, 1)))
                    for p in range(h * 4, h * 4 + 4):
                        co = p * 64
                        jobs.append((psD[EV, co:co + 64], vtm[EV, co:co + 64],
                                     lrk[EV, co:co + 64], TPE, ("Y", h, 0)))
                        jobs.append((psD[OD, co:co + 64], vtm[OD, co:co + 64],
                                     lrk[OD, co:co + 64], TPO, ("Y", h, 1)))
                    mm_group(jobs)
                    s_cp(xb_t[:, 512 + h * 256:512 + (h + 1) * 256],
                         psD[:, h * 256:(h + 1) * 256])
                    s_cp(xb_t[:, h * 256:(h + 1) * 256],
                         psA[:, h * 256:(h + 1) * 256])
                    if h == 0:
                        yield
                dma_begin()
                dma_sp(bass.AP(ylsp_h, tok0 + t0,
                               [[TC, 128], [128 * TC, PCH], [1, L]]),
                       xb_t[:, 512:1024])
                dma_sp(bass.AP(ycsp_h, tok0 + t0,
                               [[NP8 * TC, 128], [TC, NP8], [1, L]]),
                       xb_t[:, 0:512])
                dma_end()
                yield

                jobs = []
                for p in range(NP8):
                    ps, pc = (psB, p * 128) if p < 4 else (psC, (p - 4) * 128)
                    zsl = zbf[:, p * 128:(p + 1) * 128]
                    xsl = xfin[:, p * 128:(p + 1) * 128]
                    jobs.append((ps[EV, pc:pc + 128], idn[EV, 0:64],
                                 zsl[EV, :], TPE, ("Z" + str(p // 4), 0)))
                    jobs.append((ps[OD, pc:pc + 128], idn[OD, 64:128],
                                 zsl[OD, :], TPO, ("Z" + str(p // 4), 1)))
                    jobs.append((ps[EV, pc:pc + 128],
                                 btm[EV, p * 64:p * 64 + 64], xsl[EV, :], TPE,
                                 ("Z" + str(p // 4), 0)))
                    jobs.append((ps[OD, pc:pc + 128],
                                 btm[OD, p * 64:p * 64 + 64], xsl[OD, :], TPO,
                                 ("Z" + str(p // 4), 1)))
                    jobs.append((ps[EV, pc + 64:pc + 128],
                                 ktm[EV, p * 64:p * 64 + 64],
                                 vtm[EV, p * 64:p * 64 + 64], TPE,
                                 ("Z" + str(p // 4), 0)))
                    jobs.append((ps[OD, pc + 64:pc + 128],
                                 ktm[OD, p * 64:p * 64 + 64],
                                 vtm[OD, p * 64:p * 64 + 64], TPO,
                                 ("Z" + str(p // 4), 1)))
                mm_group(jobs)
                for p in range(NP8):
                    ps, pc = (psB, p * 128) if p < 4 else (psC, (p - 4) * 128)
                    v_ts(z32[:, p * 128:(p + 1) * 128], ps[:, pc:pc + 128],
                         d63[:, p * NCH + c:p * NCH + c + 1], None, AL.mult)
                s_cp(zbf[:, :], z32[:, :])
                yield

        def sweep2_loads(ti):
            tok0 = ti * TS
            pa = ti % 2
            Syl = (b_t, b_t2)[pa]
            Syc = (ksc, ksc2)[pa]
            Sv = (v_t, v_t2)[pa]
            Sg = (g_t, x2bf)[pa]

            def xg(m):
                if pa == 0:
                    return fm1(x32, m, 0, TS)
                wb = wstg if m < 4 else wstg2
                return wb[:, (m % 4) * TS:(m % 4 + 1) * TS]

            dma_begin()
            dma_in(bass.AP(Syl, 0, [[FT, 128], [TS, PCH], [1, TS]]),
                   bass.AP(ylsp_h, tok0,
                           [[TC, 128], [128 * TC, PCH], [1, TS]]))
            dma_in(bass.AP(Syc, 0, [[FT, 128], [TS, NP8], [1, TS]]),
                   bass.AP(ycsp_h, tok0,
                           [[NP8 * TC, 128], [TC, NP8], [1, TS]]))
            dma_sp(bass.AP(Sv, 0, [[FT, 128], [TS, PCH], [1, TS]]),
                   bass.AP(vsp_h, tok0,
                           [[TC, 128], [128 * TC, PCH], [1, TS]]))
            dma_sp(bass.AP(Sg, 0, [[FT1, 128], [TP1, PCH], [1, TS]]),
                   bass.AP(gsp_h, tok0,
                           [[TC, 128], [128 * TC, PCH], [1, TS]]))
            dma_end()


        def tailprep_gen():
            for _ in ffnprep_gen():
                yield

        def ffnprep_gen():
            big_weight(w_r, W_h["W_o"])
            yield
            for q in range(PCH):
                dma_in(wstg[:, :],
                       W_h["ffn_w"].ap()[q * 128:(q + 1) * 128, :])
                v_red(gmax[:, NT + 1 + q:NT + 2 + q], wstg[:, :], AL.add,
                      absv=True)
                yield
            v_red(gcol[:, 0:1], gmax[:, NT + 1:NT + 9], AL.add)
            t_mm(psF[0:1, 0:128], gcol[:, 0:1], idn32[:, :])
            v_red(srow[0:1, 0:1], psF[0:1, 0:128], AL.add)
            t_mm(psE[:, 0:1], ones_r32[0:1, :], srow[0:1, 0:1])
            v_cp(gcol[:, 1:2], psE[:, 0:1])
            v_ts(gcol[:, 1:2], gcol[:, 1:2], 1.0 / (C * C), None, AL.mult)
            v_ts(gcol[:, 2:3], gcol[:, 1:2], FFN_EPS, None, AL.add)
            v_recip(gcol[:, 3:4], gcol[:, 2:3])
            if DEBUG:
                GE.wait(s_ve, s_ve.v)
                dma_sp(dbg_h["d_gc1"].ap(), gcol[:, 0:8])
            yield

        def quant_gen():
            for q in range(PCH):
                dma_in(wstg[:, :],
                       W_h["ffn_w"].ap()[q * 128:(q + 1) * 128, :])
                v_ts(wstg[:, :], wstg[:, :], gcol[:, 3:4], MAGIC, AL.mult,
                     AL.add)
                v_ts(wstg[:, :], wstg[:, :], -MAGIC, None, AL.add)
                v_ts(w_k[:, q * C:(q + 1) * C], wstg[:, :], 1.0, -1.0, AL.min,
                     AL.max)
                yield
            if DEBUG:
                GE.wait(s_ve, s_ve.v)
                dma_sp(dbg_h["d_gc2"].ap(), gcol[:, 0:8])

        _END = object()

        def drain(g, n):
            for _ in range(n):
                if next(g, _END) is _END:
                    return False
            return True

        gw = weights_gen()
        gp0 = prep_tile(0)
        aw = ap0 = True
        while aw or ap0:
            if aw:
                aw = drain(gw, 3)
            if ap0:
                ap0 = drain(gp0, 1)
        for ti in range(NT):
            gs = scan_tile(ti)
            gp = prep_tile(ti + 1) if ti + 1 < NT else tailprep_gen()
            a1 = a2 = True
            while a1 or a2:
                if a1:
                    a1 = drain(gs, 2)
                if a2:
                    a2 = drain(gp, 1)

        ph("scan_done")
        for t in range(min(2, NT)):
            sweep2_loads(t)

        def ag1_gen():
            TE.wait(s_sc, s_sc.v)
            TE.wait(s_ve, s_ve.v)
            jobs = []
            for p in range(NP8):
                ps, pc = (psB, p * 128) if p < 4 else (psC, (p - 4) * 128)
                jobs.append((ps[EV, pc:pc + 64], z32[EV, p * 128:p * 128 + 64],
                             idn32[EV, 0:64], TPE, ("zt" + str(p // 4), 0)))
                jobs.append((ps[OD, pc:pc + 64], z32[OD, p * 128:p * 128 + 64],
                             idn32[OD, 64:128], TPO, ("zt" + str(p // 4), 1)))
            mm_group(jobs)
            mm_zt = TE.chk(s_mm)
            SE.wait(s_mm, mm_zt)
            for p in range(NP8):
                ps, pc = (psB, p * 128) if p < 4 else (psC, (p - 4) * 128)
                s_cp(agst[:, p * 128:p * 128 + 64], ps[:, pc:pc + 64])
                s_cp(agst[:, p * 128 + 64:(p + 1) * 128],
                     z32[:, p * 128 + 64:(p + 1) * 128])
            sc_ag = SE.chk(s_sc)
            GE.wait(s_sc, sc_ag)
            dma_sp(ag1_in.ap(), agst[:, 0:1024])
            yield
            GE.wait(s_sp, s_sp.v)
            cc1 = E(GE, lambda e: e.collective_compute(
                "AllGather", AL.bypass,
                replica_groups=[list(range(NCOR))],
                ins=[ag1_in.ap().opt()],
                outs=[ag1_out.ap().opt()],
            ), rd=[ag1_in.ap()], wr=[ag1_out.ap()], inc=1, asyn=True)
            v_set(z32[:, 0:512], 0.0)
            v_set(xa_t[:, 0:512], 0.0)
            GE.wait(s_cc, cc1)
            yield
            dma_sp(agrm[:, 0:1024], ag1_out.ap()[0])
            for jc in range(NCOR - 1):
                rb = (jc & 1) * 1024
                if jc + 1 < NCOR - 1:
                    nb = ((jc + 1) & 1) * 1024
                    dma_sp(agrm[:, nb:nb + 1024], ag1_out.ap()[jc + 1])
                s_cp(agrb[:, rb:rb + 1024], agrm[:, rb:rb + 1024])
                jobs = []
                for p in range(NP8):
                    jobs.append((psA[EV, p * 64:p * 64 + 64],
                                 agrb[EV, rb + p * 128:rb + p * 128 + 64],
                                 xa_t[EV, p * 64:p * 64 + 64], TPE, ("G", 0)))
                    jobs.append((psA[OD, p * 64:p * 64 + 64],
                                 agrb[OD, rb + p * 128:rb + p * 128 + 64],
                                 xa_t[OD, p * 64:p * 64 + 64], TPO, ("G", 1)))
                mm_group(jobs)
                v_tt(x32[:, 0:512], psA[:, 0:512],
                     bass.AP(agrb, rb + 64, [[FT, 128], [128, NP8], [1, 64]]),
                     AL.add)
                v_ts(x32[:, 0:512], x32[:, 0:512], sel128[:, jc:jc + 1], None, AL.mult)
                v_stt(x32[:, 512:1024], z32[:, 0:512], sel128[:, jc:jc + 1], z32[:, 0:512],
                      AL.mult, AL.subtract)
                v_tt(z32[:, 0:512], x32[:, 0:512], x32[:, 512:1024], AL.subtract)
                v_cp(xa_t[:, 0:512], z32[:, 0:512])
                yield

            if DEBUG:
                GE.wait(s_ve, s_ve.v)
                dma_sp(dbg_h["d_zin"].ap(), z32[:, 0:512])


        g1g = ag1_gen()
        g2g = quant_gen()
        a1 = a2 = True
        while a1 or a2:
            if a1:
                a1 = drain(g1g, 1)
            if a2:
                a2 = drain(g2g, 1)

        ph("ag1")

        ph("ffn_prep")

        def sweep2_tile(ti, preloaded=False):
            tok0 = ti * TS
            pa = ti % 2
            Syl = (b_t, b_t2)[pa]
            Syc = (ksc, ksc2)[pa]
            Sv = (v_t, v_t2)[pa]
            Sg = (g_t, x2bf)[pa]
            Sy = (kk_t, r_t2)[pa]
            Ssq = (dsc, xx)[pa]
            Sxn = (kq_t, kq_t2)[pa]
            Sxb = (xs, r_t)[pa]
            gb0 = pa * 1024
            So = (lg_t, ldg)[pa]
            B1, B2, B3 = ((psA, psB, psC), (psD, psE, psF))[pa]

            def xg(m):
                if pa == 0:
                    return fm1(x32, m, 0, TS)
                wb = wstg if m < 4 else wstg2
                return wb[:, (m % 4) * TS:(m % 4 + 1) * TS]

            if not preloaded:
                sweep2_loads(ti)
            dma_begin()
            for q in range(PCH):
                dma_in(xg(q),
                       xT_h.ap()[q * 128:(q + 1) * 128,
                                 tok0 + 1:tok0 + 1 + TS])
            dma_end()
            yield

            for cc in range(CPT):
                t0 = cc * L
                jobs = []
                for p in range(NP8):
                    jobs.append((B1[EV, p * 64:p * 64 + 64],
                                 xa_t[EV, p * 64:p * 64 + 64],
                                 Syc[EV, p * TS + t0:p * TS + t0 + L], TPE,
                                 ("c", 0)))
                    jobs.append((B1[OD, p * 64:p * 64 + 64],
                                 xa_t[OD, p * 64:p * 64 + 64],
                                 Syc[OD, p * TS + t0:p * TS + t0 + L], TPO,
                                 ("c", 1)))
                mm_group(jobs)
                v_tt(bass.AP(Sy, t0, [[FT, 128], [TS, NP8], [1, L]]),
                     bass.AP(B1, 0, [[512, 128], [64, NP8], [1, L]]),
                     bass.AP(Syl, t0, [[FT, 128], [TS, NP8], [1, L]]),
                     AL.add)
                if cc % 2 == 1:
                    yield

            if DEBUG:
                GE.wait(s_ve, s_ve.v)
                dma_sp(bass.AP(dbg_h["d_ycr"], tok0,
                               [[TC, 128], [128 * TC, PCH], [1, TS]]),
                       bass.AP(Sy, 0, [[FT, 128], [TS, PCH], [1, TS]]))
            for q in range(PCH):
                s_act(fm(Ssq, q, 0, TS), fm(Sy, q, 0, TS), ACT.Square)
            for q in range(PCH):
                t_mm(B2[0:16, 0:TS], hind[:, q * 16:(q + 1) * 16],
                     fm(Sy, q, 0, TS), start=(q == 0), stop=(q == PCH - 1))
            for q in range(PCH):
                t_mm(B3[0:16, 0:TS], hind[:, q * 16:(q + 1) * 16],
                     fm(Ssq, q, 0, TS), start=(q == 0), stop=(q == PCH - 1))
            yield
            v_ts(gnr[0:16, 0:TS], B2[0:16, 0:TS], 1.0 / 64, None, AL.mult)
            v_ts(gnr[0:16, TS:2 * TS], B3[0:16, 0:TS], 1.0 / 64, None,
                 AL.mult)
            s_act(gnr[0:16, 2 * TS:3 * TS], gnr[0:16, 0:TS], ACT.Square)
            v_tt(gnr[0:16, TS:2 * TS], gnr[0:16, TS:2 * TS],
                 gnr[0:16, 2 * TS:3 * TS], AL.subtract)
            s_act(gnr[0:16, 2 * TS:3 * TS], gnr[0:16, TS:2 * TS], ACT.Sqrt,
                  bias=GN_EPS)
            v_recip(gnr[0:16, 3 * TS:4 * TS], gnr[0:16, 2 * TS:3 * TS])
            v_tt(gnr[0:16, 0:TS], gnr[0:16, 0:TS], gnr[0:16, 3 * TS:4 * TS],
                 AL.mult)
            v_ts(gnr[0:16, 0:TS], gnr[0:16, 0:TS], -1.0, None, AL.mult)
            v_cp(xln[0:16, gb0:gb0 + TS], gnr[0:16, 3 * TS:4 * TS])
            v_cp(xln[0:16, gb0 + TS:gb0 + 2 * TS], gnr[0:16, 0:TS])
            yield

            for q in range(PCH):
                t_mm(B2[:, 0:TS], hexp[0:16, q * 128:(q + 1) * 128],
                     xln[0:16, gb0:gb0 + TS], start=True, stop=True)
                t_mm(B3[:, 0:TS], hexp[0:16, q * 128:(q + 1) * 128],
                     xln[0:16, gb0 + TS:gb0 + 2 * TS], start=True, stop=True)
                t_mm(B1[:, 0:TS], hexp[0:16, q * 128:(q + 1) * 128],
                     u_bf[0:16, tok0:tok0 + TS], start=True, stop=True)
                v_tt(fm(Sy, q, 0, TS), fm(Sy, q, 0, TS), B2[:, 0:TS],
                     AL.mult)
                v_tt(fm(Sy, q, 0, TS), fm(Sy, q, 0, TS), B3[:, 0:TS],
                     AL.add)
                v_ts(fm(Sy, q, 0, TS), fm(Sy, q, 0, TS),
                     par["ln_w"][:, q:q + 1], par["ln_b"][:, q:q + 1],
                     AL.mult, AL.add)
                v_tt(fm(Sxb, q, 0, TS), B1[:, 0:TS], fm(Sv, q, 0, TS),
                     AL.mult)
                v_tt(fm(Sy, q, 0, TS), fm(Sy, q, 0, TS), fm(Sxb, q, 0, TS),
                     AL.add)
                v_tt(fm(Sy, q, 0, TS), fm(Sy, q, 0, TS),
                     fm(Sg, q, 0, TS, W=TP1), AL.mult)
                if q % 2 == 1:
                    yield

            if DEBUG:
                GE.wait(s_ve, s_ve.v)
                dma_sp(bass.AP(dbg_h["d_yfin"], tok0,
                               [[TC, 128], [128 * TC, PCH], [1, TS]]),
                       bass.AP(Sy, 0, [[FT, 128], [TS, PCH], [1, TS]]))
            for mp in range(0, PCH, 2):
                ps = B2 if (mp % 4 == 0) else B3
                for m2 in (mp, mp + 1):
                    co = (m2 & 1) * TS
                    for q in range(PCH):
                        t_mm(ps[:, co:co + TS],
                             w_r[:, q * C + m2 * 128: q * C + (m2 + 1) * 128],
                             fm(Sy, q, 0, TS),
                             start=(q == 0 and co == 0),
                             stop=(q == PCH - 1 and co == TS))
                if pa == 0:
                    xpair = bass.AP(x32, mp * TP1,
                                    [[FT1, 128], [TP1, 2], [1, TS]])
                else:
                    wb = wstg if mp < 4 else wstg2
                    xpair = wb[:, (mp % 4) * TS:(mp % 4) * TS + 2 * TS]
                v_tt(So[:, mp * TS:(mp + 2) * TS], ps[:, 0:2 * TS], xpair,
                     AL.add)
                yield
            dma_sp(bass.AP(x1sp_h, tok0, [[TC, 128], [128 * TC, PCH], [1, TS]]),
                   bass.AP(So, 0, [[FT if pa == 0 else FT1, 128], [TS, PCH],
                                   [1, TS]]))
            if DEBUG:
                dma_sp(bass.AP(dbg_h["d_ot"], tok0,
                               [[TC, 128], [128 * TC, PCH], [1, TS]]),
                       bass.AP(So, 0, [[FT if pa == 0 else FT1, 128],
                                       [TS, PCH], [1, TS]]))

            s_cp(Ssq[:, 0:FT], So[:, 0:FT])
            s_act(Sxn[:, 0:FT], So[:, 0:FT], ACT.Square)
            for q in range(PCH):
                t_mm(B1[0:1, 0:TS], ones_c[:, 0:1], fm(Ssq, q, 0, TS),
                     start=(q == 0), stop=(q == PCH - 1))
            for q in range(PCH):
                t_mm(B2[0:1, 0:TS], ones_c[:, 0:1], fm(Sxn, q, 0, TS),
                     start=(q == 0), stop=(q == PCH - 1))
            yield
            v_ts(srow[0:1, 0:TS], B1[0:1, 0:TS], 1.0 / C, None, AL.mult)
            v_ts(srow[0:1, TP1:TP1 + TS], B2[0:1, 0:TS], 1.0 / C, None,
                 AL.mult)
            s_act(srow2[0:1, 0:TS], srow[0:1, 0:TS], ACT.Square)
            v_tt(srow2[0:1, TP1:TP1 + TS], srow[0:1, TP1:TP1 + TS],
                 srow2[0:1, 0:TS], AL.subtract)
            s_act(srow2[0:1, 2 * TP1:2 * TP1 + TS],
                  srow2[0:1, TP1:TP1 + TS], ACT.Sqrt, bias=LN2_EPS)
            v_recip(srow[0:1, 2 * TP1:2 * TP1 + TS],
                    srow2[0:1, 2 * TP1:2 * TP1 + TS])
            v_tt(srow2[0:1, 0:TS], srow[0:1, 0:TS],
                 srow[0:1, 2 * TP1:2 * TP1 + TS], AL.mult)
            v_ts(srow2[0:1, 0:TS], srow2[0:1, 0:TS], -1.0, None, AL.mult)
            v_cp(srowb[0:1, 0:TS], srow[0:1, 2 * TP1:2 * TP1 + TS])
            v_cp(srowb[0:1, TP1:TP1 + TS], srow2[0:1, 0:TS])
            t_mm(B3[:, 0:TS], ones_r[0:1, :], srowb[0:1, 0:TS])
            t_mm(B1[:, 0:TS], ones_r[0:1, :], srowb[0:1, TP1:TP1 + TS])
            yield
            for q in range(PCH):
                v_tt(fm(Sxn, q, 0, TS), fm(Ssq, q, 0, TS), B3[:, 0:TS],
                     AL.mult)
                v_tt(fm(Sxn, q, 0, TS), fm(Sxn, q, 0, TS), B1[:, 0:TS],
                     AL.add)
                v_ts(fm(Sxn, q, 0, TS), fm(Sxn, q, 0, TS),
                     par["ffn_g"][:, q:q + 1], par["ffn_b"][:, q:q + 1],
                     AL.mult, AL.add)
            dma_sp(bass.AP(xnsp_h, tok0, [[TC, 128], [128 * TC, PCH], [1, TS]]),
                   bass.AP(Sxn, 0, [[FT, 128], [TS, PCH], [1, TS]]))
            v_red(gmax[:, ti:ti + 1], Sxn[:, 0:FT], AL.max, absv=True)
            yield

        for tp in range(0, NT, 2):
            g1 = sweep2_tile(tp, preloaded=(tp == 0))
            g2 = (sweep2_tile(tp + 1, preloaded=(tp == 0 and NT > 1))
                  if tp + 1 < NT else None)
            a1, a2 = True, g2 is not None
            if a1:
                a1 = drain(g1, 4)
            while a1 or a2:
                if a1:
                    a1 = drain(g1, 1)
                if a2:
                    a2 = drain(g2, 1)

        def sweep3_loads(ti):
            tok0 = ti * TS
            pa = ti % 2
            Sxn = (kq_t, kq_t2)[pa]
            So = (lg_t, ldg)[pa]
            dma_begin()
            dma_in(bass.AP(Sxn, 0, [[FT, 128], [TS, PCH], [1, TS]]),
                   bass.AP(xnsp_h, tok0,
                           [[TC, 128], [128 * TC, PCH], [1, TS]]))
            dma_sp(bass.AP(So, 0, [[FT if pa == 0 else FT1, 128], [TS, PCH],
                                   [1, TS]]),
                   bass.AP(x1sp_h, tok0,
                           [[TC, 128], [128 * TC, PCH], [1, TS]]))
            dma_end()

        def sweep3_tile(ti, preloaded=False):
            tok0 = ti * TS
            pa = ti % 2
            Sxn = (kq_t, kq_t2)[pa]
            So = (lg_t, ldg)[pa]
            Sxq = (xs, r_t)[pa]
            Ba, Bb = ((psE, psF), (psB, psC))[pa]
            if not preloaded:
                sweep3_loads(ti)
            yield
            for q in range(PCH):
                v_ts(fm(Sxq, q, 0, TS), fm(Sxn, q, 0, TS), gcol[:, 7:8],
                     QB - 1e-5, AL.mult, AL.min)
                v_ts(fm(Sxq, q, 0, TS), fm(Sxq, q, 0, TS), -QB + 1e-5, None,
                     AL.max)
            if DEBUG and ti == 0:
                GE.wait(s_ve, s_ve.v)
                dma_sp(dbg_h["d_xq"].ap(),
                       bass.AP(Sxq, 0, [[FT, 128], [1, FT]]))
            yield
            for mp in range(0, PCH, 2):
                ps = Ba if (mp % 4 == 0) else Bb
                for m2 in (mp, mp + 1):
                    co = (m2 & 1) * TS
                    for q in range(PCH):
                        t_mm(ps[:, co:co + TS],
                             w_k[:, q * C + m2 * 128: q * C + (m2 + 1) * 128],
                             fm(Sxq, q, 0, TS),
                             start=(q == 0 and co == 0),
                             stop=(q == PCH - 1 and co == TS))
                v_stt(So[:, mp * TS:(mp + 2) * TS], ps[:, 0:2 * TS],
                      gcol[:, 5:6], So[:, mp * TS:(mp + 2) * TS],
                      AL.mult, AL.add)
                yield
            dma_sp(bass.AP(out_h, tok0, [[TC, 128], [128 * TC, PCH], [1, TS]]),
                   bass.AP(So, 0, [[FT if pa == 0 else FT1, 128], [TS, PCH],
                                   [1, TS]]))
            yield


        ph("sweep2")
        v_red(gcol[:, 4:5], gmax[:, 0:NT], AL.max)
        ve_gm = VE.chk(s_ve)
        if DEBUG:
            GE.wait(s_ve, ve_gm)
            dma_sp(dbg_h["d_gc3"].ap(), gcol[:, 0:8])
        GE.wait(s_ve, ve_gm)
        dma_sp(ag2_in.ap(), gcol[:, 4:5])
        GE.wait(s_sp, s_sp.v)
        cc2 = E(GE, lambda e: e.collective_compute(
            "AllGather", AL.bypass,
            replica_groups=[list(range(NCOR))],
            ins=[ag2_in.ap().opt()],
            outs=[ag2_out.ap().opt()],
        ), rd=[ag2_in.ap()], wr=[ag2_out.ap()], inc=1, asyn=True)
        GE.wait(s_cc, cc2)
        for t in range(min(2, NT)):
            sweep3_loads(t)
        gth = dma_sp(bass.AP(gmax, NT + 1, [[NT + 10, 128], [1, 8]]),
                     bass.AP(ag2_out, 0, [[8, 128], [1, 8]]))
        VE.wait(s_sp, gth)
        v_red(gcol[:, 5:6], gmax[:, NT + 1:NT + 9], AL.max)
        ve_g8 = VE.chk(s_ve)
        if DEBUG:
            GE.wait(s_ve, ve_g8)
            dma_sp(dbg_h["d_gmx"].ap(), gmax[:, 0:NT + 10])
        TE.wait(s_ve, ve_g8)
        t_mm(psA[0:1, 0:128], gcol[:, 5:6], idn32[:, :])
        mm_gt = TE.chk(s_mm)
        VE.wait(s_mm, mm_gt)
        v_red(srow[0:1, 0:1], psA[0:1, 0:128], AL.max)
        ve_gt = VE.chk(s_ve)
        TE.wait(s_ve, ve_gt)
        t_mm(psB[:, 0:1], ones_r32[0:1, :], srow[0:1, 0:1])
        mm_gb = TE.chk(s_mm)
        VE.wait(s_mm, mm_gb)
        v_cp(gcol[:, 6:7], psB[:, 0:1])
        v_recip(gcol[:, 7:8], gcol[:, 6:7])
        v_ts(gcol[:, 7:8], gcol[:, 7:8], QB, None, AL.mult)
        v_tt(gcol[:, 5:6], gcol[:, 6:7], gcol[:, 1:2], AL.mult)
        v_ts(gcol[:, 5:6], gcol[:, 5:6], 1.0 / QB, None, AL.mult)
        ve_sc = VE.chk(s_ve)
        if DEBUG:
            GE.wait(s_ve, ve_sc)
            dma_sp(dbg_h["d_gcol"].ap(), gcol[:, 0:8])
            dma_sp(dbg_h["d_wq"].ap(), w_k[:, 0:C])

        ph("ag2")

        for tp in range(0, NT, 2):
            g1 = sweep3_tile(tp, preloaded=(tp == 0))
            g2 = (sweep3_tile(tp + 1, preloaded=(tp == 0 and NT > 1))
                  if tp + 1 < NT else None)
            a1, a2 = True, g2 is not None
            while a1 or a2:
                if a1:
                    a1 = drain(g1, 1)
                if a2:
                    a2 = drain(g2, 1)
        GE.wait(s_st, s_st.v)

        ph("sweep3")
        with nc.Block() as block:
            @block.tensor
            def _(e):
                TE.run(e)

            @block.vector
            def _(e):
                VE.run(e)

            @block.scalar
            def _(e):
                SE.run(e)

            @block.gpsimd
            def _(e):
                GE.run(e)

            @block.sync
            def _(e):
                SY.run(e)

    return nc


_PROG_CACHE = {}


def _host_inputs(inputs):
    x = np.ascontiguousarray(np.asarray(inputs["L0_attn_x"],
                                        dtype=np.float32).reshape(T_ALL, C))
    xT = np.ascontiguousarray(x.T)
    xTp = np.concatenate([np.zeros((C, 1), np.float32), xT], axis=1)

    def pk(v):
        return np.ascontiguousarray(
            np.asarray(v, dtype=np.float32).reshape(-1).reshape(PCH, 128).T)

    par = {
        "p_maa_x": pk(inputs["L0_attn_time_maa_x"]),
        "p_maa_r": pk(inputs["L0_attn_time_maa_r"]),
        "p_maa_w": pk(inputs["L0_attn_time_maa_w"]),
        "p_maa_k": pk(inputs["L0_attn_time_maa_k"]),
        "p_maa_v": pk(inputs["L0_attn_time_maa_v"]),
        "p_maa_a": pk(inputs["L0_attn_time_maa_a"]),
        "p_maa_g": pk(inputs["L0_attn_time_maa_g"]),
        "p_ndecay": pk(-np.asarray(inputs["L0_attn_time_decay"],
                                   dtype=np.float32)),
        "p_aaaaa": pk(inputs["L0_attn_time_aaaaa"]),
        "p_faaaa": pk(inputs["L0_attn_time_faaaa"]),
        "p_ln_w": pk(inputs["L0_attn_ln_w"]),
        "p_ln_b": pk(inputs["L0_attn_ln_b"]),
        "p_ffn_g": pk(inputs["L0_ffn_ln_gamma"]),
        "p_ffn_b": pk(inputs["L0_ffn_ln_beta"]),
    }

    su = np.triu(np.ones((128, 64), np.float32), 1)[0:128, :]
    su = np.zeros((128, 128), np.float32)
    for r in range(128):
        for cpos in range(64):
            rr = r % 64
            su[r, cpos] = 1.0 if cpos > rr else 0.0
            su[r, 64 + cpos] = 1.0 if cpos >= rr else 0.0
    sl = np.zeros((128, 64), np.float32)
    for r in range(128):
        rr = r % 64
        sl[r, 0:rr] = 1.0
    idn = np.eye(128, dtype=np.float32)
    hd = np.zeros((128, PCH * 16), np.float32)
    hx = np.zeros((16, PCH * 128), np.float32)
    for q in range(PCH):
        for cpos in range(128):
            h = 2 * q + (cpos // 64)
            hd[cpos, q * 16 + h] = 1.0
            hx[h, q * 128 + cpos] = 1.0

    common = {
        "W_r": np.ascontiguousarray(inputs["L0_attn_W_r"], dtype=np.float32),
        "W_k": np.ascontiguousarray(inputs["L0_attn_W_k"], dtype=np.float32),
        "W_v": np.ascontiguousarray(inputs["L0_attn_W_v"], dtype=np.float32),
        "W_o": np.ascontiguousarray(inputs["L0_attn_W_o"], dtype=np.float32),
        "ffn_w": np.ascontiguousarray(inputs["L0_ffn_weight"],
                                      dtype=np.float32),
        "maa_w1": np.ascontiguousarray(inputs["L0_attn_time_maa_w1"],
                                       dtype=np.float32),
        "maa_w2": np.ascontiguousarray(inputs["L0_attn_time_maa_w2"],
                                       dtype=np.float32),
        "decay_w1": np.ascontiguousarray(inputs["L0_attn_time_decay_w1"],
                                         dtype=np.float32),
        "decay_w2": np.ascontiguousarray(inputs["L0_attn_time_decay_w2"],
                                         dtype=np.float32),
        "aaa_w1": np.ascontiguousarray(inputs["L0_attn_time_aaa_w1"],
                                       dtype=np.float32),
        "aaa_w2": np.ascontiguousarray(inputs["L0_attn_time_aaa_w2"],
                                       dtype=np.float32),
        "kkk_w1": np.ascontiguousarray(inputs["L0_attn_time_kkk_w1"],
                                       dtype=np.float32),
        "kkk_w2": np.ascontiguousarray(inputs["L0_attn_time_kkk_w2"],
                                       dtype=np.float32),
        "gate_w1": np.ascontiguousarray(inputs["L0_attn_gate_w1"],
                                        dtype=np.float32),
        "gate_w2": np.ascontiguousarray(inputs["L0_attn_gate_w2"],
                                        dtype=np.float32),
        "idn": idn, "msk_su": su, "msk_sl": sl, "hind": hd, "hexp": hx,
        "cvals": np.tile(np.array([[LN1_EPS, 1.0, -0.25, -0.5, -1.0,
                                    GN_EPS, LN2_EPS, 0.5]], np.float32),
                         (128, 1)),
    }
    common.update(par)

    in_maps = []
    for c in range(NCOR):
        m = dict(common)
        t0 = c * TC
        m["xT"] = np.ascontiguousarray(xTp[:, t0:t0 + TC + 1])
        s = np.zeros((1, 8), np.float32)
        s[0, :c] = 1.0
        m["sel"] = s
        in_maps.append(m)
    return in_maps


def kernel(**inputs):
    if "prog" not in _PROG_CACHE:
        _PROG_CACHE["prog"] = build_program()
    nc = _PROG_CACHE["prog"]
    in_maps = _host_inputs(inputs)
    res = run_bass_kernel_spmd(nc, in_maps, core_ids=list(range(NCOR)))
    outs = []
    for c in range(NCOR):
        o = np.asarray(res.results[c]["out"])
        outs.append(np.ascontiguousarray(o.T))
    full = np.concatenate(outs, axis=0)
    return full.reshape(1, T_ALL, C).astype(np.float32)


# version 16 (final)
# speedup vs baseline: 1.0200x; 1.0200x over previous
import numpy as np
import ml_dtypes
import bisect
import contextlib
import concourse.bass as bass
import concourse.mybir as mybir
from concourse.bass_utils import run_bass_kernel_spmd
from concourse.bass_isa import ReduceOp

F32 = mybir.dt.float32
BF16 = mybir.dt.bfloat16
AL = mybir.AluOpType
ACT = mybir.ActivationFunctionType
AXX = mybir.AxisListType.X
bf16 = ml_dtypes.bfloat16

NCOR = 8
C = 1024
T_ALL = 16384
TC = T_ALL // NCOR
TS = 256
TP1 = TS + 1
NT = TC // TS
H = 16
L = 64
CPT = TS // L
NCH = TC // L
PCH = 8
NP8 = 8
FT = PCH * TS
FT1 = PCH * TP1
LN1_EPS = 1e-5
GN_EPS = 1e-5 * 64.0
LN2_EPS = 1e-3
FFN_EPS = 1e-5
QB = 128.0
MAGIC = float(np.float32(1.5 * 2.0 ** 23))
DEBUG = False
PHASES = []
EV = slice(0, 64)
OD = slice(64, 128)
TPE = (0, 0)
TPO = (64, 64)
NSOLVE = 6


class Sem:
    def __init__(self, h, ooo=False):
        self.h = h
        self.v = 0
        self.ooo = ooo
        self.km = []
        self.kd = []


class Op:
    __slots__ = ("fn", "incs", "iswait")

    def __init__(self, fn, iswait=False):
        self.fn = fn
        self.incs = []
        self.iswait = iswait


class Eng:

    def __init__(self, name, sem, dinc=1):
        self.name = name
        self.q = []
        self.sem = sem
        self.dinc = dinc
        self.waited = {}

    def op(self, fn, inc=None, sem=None):
        o = Op(fn)
        self.q.append(o)
        s = self.sem if sem is None else sem
        n = self.dinc if inc is None else inc
        o.incs.append((s, n))
        s.v += n
        if self.waited:
            s.km.append(s.v)
            s.kd.append(dict(self.waited))
        return s.v

    def chk(self, sem, n=None):
        return sem.v

    def wait(self, sem, v=None):
        val = sem.v if v is None else v
        if val <= 0:
            return
        k = id(sem)
        if self.waited.get(k, 0) >= val:
            return
        self.waited[k] = val
        self.q.append(Op(lambda e, s=sem, vv=val: e.wait_ge(s.h, vv),
                         iswait=True))
        if sem.km:
            i = bisect.bisect_right(sem.km, val) - 1
            if i >= 0:
                for kk, vv in sem.kd[i].items():
                    if self.waited.get(kk, 0) < vv:
                        self.waited[kk] = vv

    def run(self, e):
        for o in self.q:
            r = o.fn(e)
            for (sem, n) in o.incs:
                r.then_inc(sem.h, n)


def build_program():
    PHASES.clear()
    nc = bass.Bass("TRN2", target_bir_lowering=False,
                   dynamic_dma_scratch_size=4096)

    def I(name, shape, dtype=F32):
        return nc.dram_tensor(name, list(shape), dtype, kind="ExternalInput")

    xT_h = I("xT", (C, TC + 1))
    sel_h = I("sel", (1, 8))
    W_h = {k: I(k, (C, C)) for k in ("W_r", "W_k", "W_v", "W_o", "ffn_w")}
    maa1_h = I("maa_w1", (C, 192))
    maa2_h = I("maa_w2", (6, 32, C))
    d1_h = I("decay_w1", (C, 64)); d2_h = I("decay_w2", (64, C))
    a1_h = I("aaa_w1", (C, 64)); a2_h = I("aaa_w2", (64, C))
    k1_h = I("kkk_w1", (C, 64)); k2_h = I("kkk_w2", (64, C))
    g1_h = I("gate_w1", (C, 128)); g2_h = I("gate_w2", (128, C))
    PNAMES = ("maa_x", "maa_r", "maa_w", "maa_k", "maa_v", "maa_a", "maa_g",
              "ndecay", "aaaaa", "faaaa", "ln_w", "ln_b", "ffn_g", "ffn_b")
    p_h = {nm: I("p_" + nm, (128, PCH)) for nm in PNAMES}
    idn_h = I("idn", (128, 128))
    msu_h = I("msk_su", (128, 128))
    msl_h = I("msk_sl", (128, 64))
    hind_h = I("hind", (128, PCH * 16))
    hexp_h = I("hexp", (16, PCH * 128))
    cval_h = I("cvals", (128, 8))

    out_h = nc.dram_tensor("out", [C, TC], F32, kind="ExternalOutput")

    def DD(name, shape, dtype=BF16):
        kind = "ExternalOutput" if DEBUG else "Internal"
        return nc.dram_tensor(name, list(shape), dtype, kind=kind)

    vsp_h = DD("v_sp", (C, TC))
    gsp_h = DD("g_sp", (C, TC))
    ylsp_h = DD("yl_sp", (C, TC))
    ycsp_h = DD("yc_sp", (128, NP8 * TC))
    x1sp_h = DD("x1_sp", (C, TC), F32)
    xnsp_h = DD("xn_sp", (C, TC))
    dbg_h = {}
    if DEBUG:
        dbg_h["d_zin"] = nc.dram_tensor("d_zin", [128, NP8 * 64], F32,
                                        kind="ExternalOutput")
        dbg_h["d_gcol"] = nc.dram_tensor("d_gcol", [128, 8], F32,
                                         kind="ExternalOutput")
        dbg_h["d_gc1"] = nc.dram_tensor("d_gc1", [128, 8], F32,
                                        kind="ExternalOutput")
        dbg_h["d_gc2"] = nc.dram_tensor("d_gc2", [128, 8], F32,
                                        kind="ExternalOutput")
        dbg_h["d_gc3"] = nc.dram_tensor("d_gc3", [128, 8], F32,
                                        kind="ExternalOutput")
        dbg_h["d_gmx"] = nc.dram_tensor("d_gmx", [128, NT + 10], F32,
                                        kind="ExternalOutput")
        dbg_h["d_wq"] = nc.dram_tensor("d_wq", [128, 1024], BF16,
                                       kind="ExternalOutput")
        dbg_h["d_xq"] = nc.dram_tensor("d_xq", [128, 2048], BF16,
                                       kind="ExternalOutput")
        for nm in ("d_xln", "d_r", "d_kq", "d_kk", "d_b", "d_ld", "d_af",
                   "d_eng", "d_ycr", "d_ygn", "d_yfin", "d_x32", "d_ot"):
            dt = F32 if nm in ("d_ld", "d_x32", "d_ot") else BF16
            dbg_h[nm] = nc.dram_tensor(nm, [C, TC], dt, kind="ExternalOutput")
    ag1_in = nc.dram_tensor("ag1_in", [128, NP8 * 128], F32, kind="Internal")
    ag1_out = nc.dram_tensor("ag1_out", [NCOR, 128, NP8 * 128], F32,
                             kind="Internal")
    ag2_in = nc.dram_tensor("ag2_in", [128, 1], F32, kind="Internal")
    ag2_out = nc.dram_tensor("ag2_out", [NCOR, 128, 1], F32, kind="Internal")

    ctx = contextlib.ExitStack()

    def SB(name, p, f, dtype=BF16):
        return ctx.enter_context(nc.sbuf_tensor(name, [p, f], dtype))

    def SEM(name, ooo=False):
        return Sem(ctx.enter_context(nc.semaphore(name)), ooo=ooo)

    with ctx:
        w_r = SB("w_r", 128, PCH * C)
        w_k = SB("w_k", 128, PCH * C)
        w_v = SB("w_v", 128, PCH * C)
        w_m1 = SB("w_m1", 128, PCH * 192)
        w_m2 = SB("w_m2", 128, 2 * C)
        w_d1 = SB("w_d1", 128, PCH * 64)
        w_d2 = SB("w_d2", 64, C)
        w_a1 = SB("w_a1", 128, PCH * 64)
        w_a2 = SB("w_a2", 64, C)
        w_k1 = SB("w_k1", 128, PCH * 64)
        w_k2 = SB("w_k2", 64, C)
        w_g1 = SB("w_g1", 128, PCH * 128)
        w_g2 = SB("w_g2", 128, C)
        wstg = SB("wstg", 128, C, F32)
        wstg2 = SB("wstg2", 128, C, F32)

        par = {nm: SB("pp_" + nm, 128, PCH, F32) for nm in PNAMES}
        idn = SB("idn_b", 128, 128)
        idn32 = SB("idn_f", 128, 128, F32)
        msu = SB("msu_s", 128, 128, F32)
        msl = SB("msl_s", 128, 64, F32)
        hind = SB("hind_s", 128, PCH * 16)
        hexp = SB("hexp_s", 16, PCH * 128)
        ones_r = SB("ones_r", 1, 128)
        ones_c = SB("ones_c", 128, 1)
        ones_r32 = SB("ones_r32", 1, 128, F32)
        sel128 = SB("sel128", 128, 8, F32)
        cval = SB("cval", 128, 8, F32)
        CVALS = [LN1_EPS, 1.0, -0.25, -0.5, -1.0, GN_EPS, LN2_EPS, 0.5]
        for _i, _v in enumerate(CVALS):
            nc.const_aps.register(F32, float(_v), cval[:, _i:_i + 1])

        x32 = SB("x32", 128, FT1, F32)
        ldrel = x32
        xbf = SB("xbf", 128, FT1)
        g_t = xbf
        x2bf = SB("x2bf", 128, FT1)
        epg = x2bf
        en_t = SB("en_t", 128, FT1)
        xln = SB("xln", 128, FT1)
        xx = SB("xx", 128, FT)
        dsc = SB("dsc", 128, FT)
        xs = SB("xs", 128, FT)
        xs2 = SB("xs2", 128, FT)
        ksc = SB("ksc", 128, FT)
        r_t = SB("r_t", 128, FT)
        kq_t = SB("kq_t", 128, FT)
        kk_t = SB("kk_t", 128, FT)
        b_t = SB("b_t", 128, FT)
        v_t = SB("v_t", 128, FT)
        ksc2 = SB("ksc2", 128, FT)
        r_t2 = SB("r_t2", 128, FT)
        kq_t2 = SB("kq_t2", 128, FT)
        b_t2 = SB("b_t2", 128, FT)
        v_t2 = SB("v_t2", 128, FT)
        lg_t = SB("lg_t", 128, FT, F32)
        ldg = SB("ldg", 128, FT1, F32)
        bases = SB("bases", 128, PCH * CPT, F32)
        mixs = SB("mixs", 128, 2 * TS)
        hid64 = SB("hid64", 64, TS)
        hid128 = SB("hid128", 128, TS)
        srow = SB("srow", 1, 3 * TP1, F32)
        srow2 = SB("srow2", 1, 3 * TP1, F32)
        srowb = SB("srowb", 1, 2 * TP1)
        bcA = SB("bcA", 128, TP1)
        bcB = SB("bcB", 128, TP1)
        u_bf = SB("u_bf", 16, TC)
        d63 = SB("d63", 128, PCH * NCH, F32)

        vtm = SB("vtm", 128, NP8 * 64)
        btm = SB("btm", 128, NP8 * 64)
        ktm = SB("ktm", 128, NP8 * 64)
        m_sl = SB("m_sl", 128, NP8 * 64)
        m_su = SB("m_su", 128, NP8 * 64)
        t1l = SB("t1l", 128, NP8 * 64)
        lrb = SB("lrb", 128, NP8 * 64)
        lrk = SB("lrk", 128, NP8 * 64)
        pj_a = SB("pj_a", 128, NP8 * 64)
        nj_a = SB("nj_a", 128, NP8 * 64)
        pj_b = SB("pj_b", 128, NP8 * 64)
        nj_b = SB("nj_b", 128, NP8 * 64)
        xa_t = SB("xa_t", 128, NP8 * 128)
        xb_t = SB("xb_t", 128, NP8 * 128)
        z32 = SB("z32", 128, NP8 * 128, F32)
        zbf = SB("zbf", 128, NP8 * 128)
        zin = None
        zinb = None
        agst = lg_t
        agrm = agst
        agrb = dsc
        ztmp = None
        ztmp2 = None

        y_t = kk_t
        yst = None
        cst = None
        o_t = lg_t
        xn_t = kq_t
        sqb = dsc
        gnr = ldg
        gmax = SB("gmax", 128, NT + 10, F32)
        gcol = SB("gcol", 128, 8, F32)

        psA = ctx.enter_context(nc.psum_tensor([128, 512], F32))
        psB = ctx.enter_context(nc.psum_tensor([128, 512], F32))
        psC = ctx.enter_context(nc.psum_tensor([128, 512], F32))
        psD = ctx.enter_context(nc.psum_tensor([128, 512], F32))
        psE = ctx.enter_context(nc.psum_tensor([128, 512], F32))
        psF = ctx.enter_context(nc.psum_tensor([128, 512], F32))
        psG = ctx.enter_context(nc.psum_tensor([128, 512], F32))
        psH = ctx.enter_context(nc.psum_tensor([128, 512], F32))

        s_in = SEM("s_in", ooo=True)
        s_sp = SEM("s_sp", ooo=True)
        s_mm = SEM("s_mm")
        s_ve = SEM("s_ve")
        s_sc = SEM("s_sc")
        s_cc = SEM("s_cc", ooo=True)
        s_gp = s_sp
        s_st = s_sp

        TE = Eng("tensor", s_mm)
        VE = Eng("vector", s_ve)
        SE = Eng("scalar", s_sc)
        GE = Eng("gpsimd", s_cc, dinc=1)
        SY = Eng("sync", s_in, dinc=16)
        ALL_ENG = (TE, VE, SE, GE, SY)

        def ph(label):
            PHASES.append((label, {e.name: len(e.q) for e in ALL_ENG}))

        MEM_W = {}
        MEM_R = {}
        ASYNC = object()

        def _span(ap):
            if isinstance(ap, tuple):
                return ap
            off = ap.offset
            tot = 0
            for st, ct in ap.ap:
                tot += (ct - 1) * abs(st)
            return ap.tensor.name, off, off + tot

        def E(eng, fn, rd=(), wr=(), gid=None, inc=None, sem=None,
              asyn=False):
            rds = [_span(a) for a in rd]
            wrs = [_span(a) for a in wr]
            def edge(sem, mark):
                eng.wait(sem, mark)

            for nm, lo, hi in rds:
                for w in MEM_W.get(nm, ()):
                    if w[0] <= hi and lo <= w[1]:
                        if gid is not None and w[2] is eng and w[5] == gid:
                            continue
                        edge(w[3], w[4])
            for nm, lo, hi in wrs:
                for w in MEM_W.get(nm, ()):
                    if w[0] <= hi and lo <= w[1]:
                        if gid is not None and w[2] is eng and w[5] == gid:
                            continue
                        edge(w[3], w[4])
                for r in MEM_R.get(nm, ()):
                    if r[0] <= hi and lo <= r[1] and (asyn
                                                      or r[2] is not eng):
                        edge(r[3], r[4])
            mark = eng.op(fn, inc=inc, sem=sem)
            msem = eng.sem if sem is None else sem
            reng = ASYNC if asyn else eng
            for nm, lo, hi in rds:
                rec = [lo, hi, reng, msem, mark]
                MEM_R.setdefault(nm, []).append(rec)
                if asyn and BATCH[0] is not None:
                    BATCH[0].append(rec)
            for nm, lo, hi in wrs:
                lw = MEM_W.setdefault(nm, [])
                lw[:] = [w for w in lw if not (lo <= w[0] and w[1] <= hi)]
                rec = [lo, hi, reng, msem, mark, gid]
                lw.append(rec)
                if asyn and BATCH[0] is not None:
                    BATCH[0].append(rec)
                lr = MEM_R.get(nm)
                if lr:
                    lr[:] = [r for r in lr if not (lo <= r[0] and r[1] <= hi)]
            return mark

        open_groups = []
        gid_ctr = [0]

        def _psum_row_span(sp):
            nm, lo, hi = sp
            return (nm, (lo // 512) * 512, (hi // 512) * 512 + 511)

        def t_mm(o, lh, rh, start=True, stop=True, tp=None):
            sp = _span(o)
            row = _psum_row_span(sp)
            if start:
                gid_ctr[0] += 1
                gid = gid_ctr[0]
                open_groups[:] = [g for g in open_groups
                                  if not (g[0][0] == row[0]
                                          and g[0][1] <= row[2]
                                          and row[1] <= g[0][2])]
                open_groups.append((row, gid))
                wsp = row
            else:
                gid = None
                for (gi, gg) in open_groups:
                    if gi[0] == sp[0] and gi[1] <= sp[2] and sp[1] <= gi[2]:
                        gid = gg
                        break
                wsp = sp
            E(TE,
              lambda e, oo=o, ll=lh, rr=rh, s0=start, s1=stop, t0=tp:
              e.matmul(oo, ll, rr, start=s0, stop=s1, tile_position=t0,
                       skip_group_check=True),
              rd=[lh, rh], wr=[wsp], gid=gid)

        def mm_group(jobs):
            firsts, lasts = {}, {}
            for i, j in enumerate(jobs):
                k = j[4]
                firsts.setdefault(k, i)
                lasts[k] = i
            for i, (o, lh, rh, tp, k) in enumerate(jobs):
                t_mm(o, lh, rh, start=(firsts[k] == i), stop=(lasts[k] == i),
                     tp=tp)

        def v_tt(o, a, b, op):
            E(VE, lambda e, oo=o, aa=a, bb=b, pp=op:
              e.tensor_tensor(oo, aa, bb, pp), rd=[a, b], wr=[o])

        def v_ts(o, a, s1, s2, op0, op1=AL.bypass):
            rd = [a]
            if hasattr(s1, "tensor"):
                rd.append(s1)
            if hasattr(s2, "tensor"):
                rd.append(s2)
            E(VE, lambda e, oo=o, aa=a, x1=s1, x2=s2, p0=op0, p1=op1:
              e.tensor_scalar(oo, aa, x1, x2, p0, p1), rd=rd, wr=[o])

        def v_stt(o, a, s, b, op0, op1):
            rd = [a, b]
            if hasattr(s, "tensor"):
                rd.append(s)
            E(VE, lambda e, oo=o, aa=a, ss=s, bb=b, p0=op0, p1=op1:
              e.scalar_tensor_tensor(oo, aa, ss, bb, p0, p1), rd=rd, wr=[o])

        def v_cp(o, a):
            E(VE, lambda e, oo=o, aa=a: e.tensor_copy(oo, aa),
              rd=[a], wr=[o])

        def v_red(o, a, op, absv=False):
            E(VE, lambda e, oo=o, aa=a, pp=op, av=absv:
              e.tensor_reduce(oo, aa, AXX, pp, apply_absolute_value=av),
              rd=[a], wr=[o])

        def v_recip(o, a):
            E(VE, lambda e, oo=o, aa=a: e.reciprocal(oo, aa),
              rd=[a], wr=[o])

        def v_set(o, val):
            E(VE, lambda e, oo=o, vv=val: e.memset(oo, vv), wr=[o])

        def s_act(o, a, fn, bias=0.0, scale=1.0):
            rd = [a]
            if hasattr(bias, "tensor"):
                rd.append(bias)
            E(SE, lambda e, oo=o, aa=a, ff=fn, bb=bias, ss=scale:
              e.activation(oo, aa, ff, bias=bb, scale=ss), rd=rd, wr=[o])

        def s_cp(o, a):
            s_act(o, a, ACT.Copy)

        def fm(t, q, c0, w, W=TS):
            return t[:, q * W + c0: q * W + c0 + w]

        def fm1(t, q, c0, w):
            return t[:, q * TP1 + c0: q * TP1 + c0 + w]

        BATCH = [None]

        def dma_begin():
            SY.wait(s_in, s_in.v)
            SE.wait(s_sp, s_sp.v)
            BATCH[0] = []

        def dma_end():
            for rec in BATCH[0]:
                rec[4] = rec[3].v
            BATCH[0] = None

        def dma_in(dst, src):
            if BATCH[0] is None:
                SY.wait(s_in, s_in.v)
            return E(SY, lambda e, dd=dst, ss=src: e.dma_start(dd, ss),
                     rd=[src], wr=[dst], asyn=True)

        def dma_sp(dst, src, slow=False):
            if BATCH[0] is None:
                SE.wait(s_sp, s_sp.v)
            return E(SE,
                     lambda e, dd=dst, ss=src, sl=slow:
                     e.dma_start(dd, ss, allow_slow_non_contiguous=sl)
                     if sl else e.dma_start(dd, ss),
                     rd=[src], wr=[dst], sem=s_sp, inc=16, asyn=True)

        for nm in PNAMES:
            dma_in(par[nm][:, :], p_h[nm].ap())
        dma_in(idn32[:, :], idn_h.ap())
        dma_in(msu[:, :], msu_h.ap())
        dma_in(msl[:, :], msl_h.ap())
        dma_in(cval[:, :], cval_h.ap())
        in_consts = dma_in(sel128[0:1, 0:8], sel_h.ap())

        SE.wait(s_in, in_consts)
        s_cp(idn[:, :], idn32[:, :])
        s_cp(srowb[0:1, 0:8], sel128[0:1, 0:8])
        sc_idn = SE.chk(s_sc)

        v_set(ones_r[:, :], 1.0)
        v_set(ones_c[:, :], 1.0)
        v_set(ones_r32[:, :], 1.0)
        v_set(gcol[:, :], 0.0)
        v_set(gmax[:, :], 0.0)
        v_set(ldg[:, :], 0.0)
        v_set(z32[:, :], 0.0)
        ve_consts = VE.chk(s_ve)

        wbufs = (wstg, wstg2)
        wb_i = [0]

        def stage(dst, src_ap, rows, cols):
            wb = wbufs[wb_i[0] & 1]
            wb_i[0] += 1
            dma_in(wb[0:rows, 0:cols], src_ap)
            s_cp(dst, wb[0:rows, 0:cols])

        for q in range(PCH):
            stage(w_m1[:, q * 192:(q + 1) * 192],
                  maa1_h.ap()[q * 128:(q + 1) * 128, :], 128, 192)
        for s in range(6):
            rb, cb = 32 * (s % 3), (s // 3) * C
            stage(w_m2[rb:rb + 32, cb:cb + C], maa2_h.ap()[s, :, :], 32, C)

        def big_weight(dst, hbm):
            for q in range(PCH):
                stage(dst[:, q * C:(q + 1) * C],
                      hbm.ap()[q * 128:(q + 1) * 128, :], 128, C)

        def weights_gen():
            for q in range(PCH):
                stage(w_d1[:, q * 64:(q + 1) * 64],
                      d1_h.ap()[q * 128:(q + 1) * 128, :], 128, 64)
                stage(w_a1[:, q * 64:(q + 1) * 64],
                      a1_h.ap()[q * 128:(q + 1) * 128, :], 128, 64)
                yield
                stage(w_k1[:, q * 64:(q + 1) * 64],
                      k1_h.ap()[q * 128:(q + 1) * 128, :], 128, 64)
                stage(w_g1[:, q * 128:(q + 1) * 128],
                      g1_h.ap()[q * 128:(q + 1) * 128, :], 128, 128)
                yield
            stage(w_d2[:, :], d2_h.ap(), 64, C)
            stage(w_a2[:, :], a2_h.ap(), 64, C)
            yield
            stage(w_k2[:, :], k2_h.ap(), 64, C)
            stage(w_g2[:, :], g2_h.ap(), 128, C)
            yield
            stage(hind[:, :], hind_h.ap(), 128, PCH * 16)
            stage(hexp[0:16, :], hexp_h.ap(), 16, PCH * 128)
            yield
            for wdst, wsrc in ((w_k, W_h["W_k"]), (w_v, W_h["W_v"]),
                               (w_r, W_h["W_r"])):
                for q in range(PCH):
                    stage(wdst[:, q * C:(q + 1) * C],
                          wsrc.ap()[q * 128:(q + 1) * 128, :], 128, C)
                    yield

        TE.wait(s_sc, sc_idn)
        TE.wait(s_ve, ve_consts)
        t_mm(psA[:, 0:8], ones_r[0:1, :], srowb[0:1, 0:8])
        mm_sel = TE.chk(s_mm)
        VE.wait(s_mm, mm_sel)
        v_cp(sel128[:, 0:8], psA[:, 0:8])
        VE.chk(s_ve)

        SE.wait(s_ve, ve_consts)
        for p in range(NP8):
            s_cp(z32[EV, p * 128:p * 128 + 64], idn32[EV, 0:64])
            s_cp(z32[OD, p * 128:p * 128 + 64], idn32[OD, 64:128])
        sc_z = SE.chk(s_sc)
        VE.wait(s_sc, sc_z)
        v_cp(zbf[:, :], z32[:, :])
        VE.chk(s_ve)
        init_done_sc = s_sc.v
        init_done_ve = s_ve.v

        ph("p0_weights")

        v_tB = (v_t, v_t2)
        b_tB = (b_t, b_t2)
        kq_tB = (kq_t, kq_t2)
        r_tB = (r_t, r_t2)
        kscB = (ksc, ksc2)

        def prep_tile(ti):
            tok0 = ti * TS
            pa = ti % 2
            Pv, Pb, Pk, Pr, Pks = (v_tB[pa], b_tB[pa], kq_tB[pa], r_tB[pa],
                                   kscB[pa])
            ph("tile_t%d" % ti)

            dma_in(bass.AP(x32, 0, [[FT1, 128], [TP1, PCH], [1, TP1]]),
                   bass.AP(xT_h, tok0,
                           [[TC + 1, 128], [128 * (TC + 1), PCH], [1, TP1]]))
            s_cp(xbf[:, :], x32[:, :])
            s_act(x2bf[:, :], x32[:, :], ACT.Square)
            yield

            for q in range(PCH):
                t_mm(psE[0:1, 0:TP1], ones_c[:, 0:1], fm1(xbf, q, 0, TP1),
                     start=(q == 0), stop=(q == PCH - 1))
            for q in range(PCH):
                t_mm(psF[0:1, 0:TP1], ones_c[:, 0:1], fm1(x2bf, q, 0, TP1),
                     start=(q == 0), stop=(q == PCH - 1))
            v_ts(srow[0:1, 0:TP1], psE[0:1, 0:TP1], 1.0 / C, None, AL.mult)
            v_ts(srow[0:1, TP1:2 * TP1], psF[0:1, 0:TP1], 1.0 / C, None,
                 AL.mult)
            s_act(srow2[0:1, 0:TP1], srow[0:1, 0:TP1], ACT.Square)
            v_tt(srow2[0:1, TP1:2 * TP1], srow[0:1, TP1:2 * TP1],
                 srow2[0:1, 0:TP1], AL.subtract)
            s_act(srow2[0:1, 2 * TP1:3 * TP1], srow2[0:1, TP1:2 * TP1],
                  ACT.Sqrt, bias=LN1_EPS)
            v_recip(srow[0:1, 2 * TP1:3 * TP1], srow2[0:1, 2 * TP1:3 * TP1])
            v_tt(srow2[0:1, 0:TP1], srow[0:1, 0:TP1],
                 srow[0:1, 2 * TP1:3 * TP1], AL.mult)
            v_ts(srow2[0:1, 0:TP1], srow2[0:1, 0:TP1], -1.0, None, AL.mult)
            v_cp(srowb[0:1, 0:TP1], srow[0:1, 2 * TP1:3 * TP1])
            v_cp(srowb[0:1, TP1:2 * TP1], srow2[0:1, 0:TP1])
            yield

            t_mm(psE[:, 0:256], ones_r[0:1, :], srowb[0:1, 0:256])
            t_mm(psE[:, 256:257], ones_r[0:1, :], srowb[0:1, 256:257])
            t_mm(psF[:, 0:256], ones_r[0:1, :], srowb[0:1, TP1:TP1 + 256])
            t_mm(psF[:, 256:257], ones_r[0:1, :],
                 srowb[0:1, TP1 + 256:TP1 + 257])
            s_cp(bcA[:, 0:TP1], psE[:, 0:TP1])
            s_cp(bcB[:, 0:TP1], psF[:, 0:TP1])
            for q in range(PCH):
                v_tt(fm1(xln, q, 0, TP1), fm1(xbf, q, 0, TP1), bcA[:, 0:TP1],
                     AL.mult)
            for q in range(PCH):
                v_tt(fm1(xln, q, 0, TP1), fm1(xln, q, 0, TP1), bcB[:, 0:TP1],
                     AL.add)
            for q in range(PCH):
                v_tt(fm(xx, q, 0, TS), fm1(xln, q, 0, TS), fm1(xln, q, 1, TS),
                     AL.subtract)
            yield

            for q in range(PCH):
                v_stt(fm(xs, q, 0, TS), fm(xx, q, 0, TS),
                      par["maa_x"][:, q:q + 1], fm1(xln, q, 1, TS),
                      AL.mult, AL.add)
            for q in range(PCH):
                t_mm(psE[0:96, 0:TS], w_m1[:, q * 192:q * 192 + 96],
                     fm(xs, q, 0, TS), start=(q == 0), stop=(q == PCH - 1))
            for q in range(PCH):
                t_mm(psF[0:96, 0:TS], w_m1[:, q * 192 + 96:(q + 1) * 192],
                     fm(xs, q, 0, TS), start=(q == 0), stop=(q == PCH - 1))
            s_act(mixs[0:96, 0:TS], psE[0:96, 0:TS], ACT.Tanh)
            s_act(mixs[0:96, TS:2 * TS], psF[0:96, 0:TS], ACT.Tanh)
            yield

            def branch_lerp(sname, sidx, xsb):
                rb = 32 * (sidx % 3)
                cb = (sidx // 3) * TS
                for mp in range(0, PCH, 2):
                    ps = psE if (mp % 4 == 0) else psF
                    for m2 in (mp, mp + 1):
                        co = (m2 & 1) * TS
                        t_mm(ps[:, co:co + TS],
                             w_m2[rb:rb + 32,
                                  (sidx // 3) * C + m2 * 128:
                                  (sidx // 3) * C + (m2 + 1) * 128],
                             mixs[rb:rb + 32, cb:cb + TS],
                             start=(co == 0), stop=(co == TS), tp=(rb, 0))
                    s_cp(dsc[:, mp * TS:(mp + 2) * TS], ps[:, 0:2 * TS])
                for q in range(PCH):
                    v_stt(fm(xsb, q, 0, TS), fm(xx, q, 0, TS),
                          par["maa_" + sname][:, q:q + 1], fm1(xln, q, 1, TS),
                          AL.mult, AL.add)
                v_tt(dsc[:, 0:FT], dsc[:, 0:FT], xx[:, 0:FT], AL.mult)
                v_tt(xsb[:, 0:FT], xsb[:, 0:FT], dsc[:, 0:FT], AL.add)

            def big_proj(wt, evac, xsb):
                for mp in range(0, PCH, 2):
                    ps = psE if (mp % 4 == 0) else psF
                    for m2 in (mp, mp + 1):
                        co = (m2 & 1) * TS
                        for q in range(PCH):
                            t_mm(ps[:, co:co + TS],
                                 wt[:, q * C + m2 * 128:
                                    q * C + (m2 + 1) * 128],
                                 fm(xsb, q, 0, TS),
                                 start=(q == 0 and co == 0),
                                 stop=(q == PCH - 1 and co == TS))
                    evac(mp, ps)

            branch_lerp("w", 1, xs)
            for q in range(PCH):
                t_mm(psE[0:64, 0:TS], w_d1[:, q * 64:(q + 1) * 64],
                     fm(xs, q, 0, TS), start=(q == 0), stop=(q == PCH - 1))
            s_act(hid64[0:64, 0:TS], psE[0:64, 0:TS], ACT.Tanh)
            yield
            for m in range(PCH):
                t_mm(psF[:, 0:TS], w_d2[0:64, m * 128:(m + 1) * 128],
                     hid64[0:64, 0:TS], start=True, stop=True)
                s_act(fm(lg_t, m, 0, TS), psF[:, 0:TS], ACT.Exp,
                      bias=par["ndecay"][:, m:m + 1], scale=-1.0)
            s_act(lg_t[:, :], lg_t[:, :], ACT.Ln, bias=1.0)
            s_act(Pks[:, 0:FT], lg_t[:, :], ACT.Exp, bias=-0.25, scale=-0.5)
            s_act(lg_t[:, :], lg_t[:, :], ACT.Exp, bias=-0.5, scale=-1.0)
            yield

            v_set(bass.AP(ldg, 0, [[TP1 * PCH, 128], [TP1, PCH]]), 0.0)
            for q in range(PCH):
                E(VE, lambda e, qq=q: e.tensor_tensor_scan(
                    fm1(ldg, qq, 1, TS),
                    bass.AP(ones_c, 0, [[1, 128], [0, TS]]),
                    fm(lg_t, qq, 0, TS), 0.0, AL.mult, AL.subtract),
                  rd=[bass.AP(ones_c, 0, [[1, 128], [0, TS]]),
                      fm(lg_t, q, 0, TS), fm1(ldg, q, 0, 1)],
                  wr=[fm1(ldg, q, 1, TS)])
            v_cp(bases[:, :],
                 bass.AP(ldg, 0, [[TP1 * PCH, 128], [TP1, PCH], [64, CPT]]))
            for q in range(PCH):
                v_tt(fm1(ldrel, q, 0, TS), fm1(ldg, q, 1, TS),
                     bass.AP(bases, q * CPT,
                             [[PCH * CPT, 128], [1, CPT], [0, 64]]),
                     AL.subtract)
            yield
            s_act(bass.AP(d63, ti * CPT,
                          [[PCH * NCH, 128], [NCH, PCH], [1, CPT]]),
                  bass.AP(ldrel, 63, [[TP1 * PCH, 128], [TP1, PCH], [64, CPT]]),
                  ACT.Exp)
            for q in range(PCH):
                s_act(fm1(epg, q, 1, TS), fm1(ldrel, q, 0, TS), ACT.Exp)
                s_act(fm1(en_t, q, 1, TS), fm1(ldrel, q, 0, TS), ACT.Exp,
                      scale=-1.0)
            v_set(bass.AP(epg, 0, [[TP1 * PCH, 128], [TP1, PCH]]), 1.0)
            yield

            branch_lerp("k", 2, xs2)
            for q in range(PCH):
                t_mm(psE[0:64, 0:TS], w_k1[:, q * 64:(q + 1) * 64],
                     fm(xs2, q, 0, TS), start=(q == 0), stop=(q == PCH - 1))
            s_act(hid64[0:64, 0:TS], psE[0:64, 0:TS], ACT.Tanh)
            yield

            def evac_k(m, ps):
                v_tt(Pk[:, m * TS:(m + 2) * TS], ps[:, 0:2 * TS],
                     Pks[:, m * TS:(m + 2) * TS], AL.mult)
                pso = psF if ps is psE else psE
                t_mm(pso[:, 0:TS], w_k2[0:64, m * 128:(m + 1) * 128],
                     hid64[0:64, 0:TS], start=True, stop=True)
                t_mm(pso[:, TS:2 * TS],
                     w_k2[0:64, (m + 1) * 128:(m + 2) * 128],
                     hid64[0:64, 0:TS], start=True, stop=True)
                s_cp(hid128[:, 0:TS], pso[:, 0:TS])
                v_tt(fm(kk_t, m, 0, TS), ps[:, 0:TS], hid128[:, 0:TS], AL.add)
                s_cp(hid128[:, 0:TS], pso[:, TS:2 * TS])
                v_tt(fm(kk_t, m + 1, 0, TS), ps[:, TS:2 * TS],
                     hid128[:, 0:TS], AL.add)

            big_proj(w_k, evac_k, xs2)
            yield

            for q in range(PCH):
                s_act(fm(sqb, q, 0, TS), fm(kk_t, q, 0, TS), ACT.Square)
            for q in range(PCH):
                t_mm(psE[0:1, 0:TS], ones_c[:, 0:1], fm(sqb, q, 0, TS),
                     start=(q == 0), stop=(q == PCH - 1))
            s_act(srow[0:1, 0:TS], psE[0:1, 0:TS], ACT.Sqrt)
            v_ts(srow[0:1, 0:TS], srow[0:1, 0:TS], 1e-12, None, AL.max)
            v_recip(srow[0:1, TS:2 * TS], srow[0:1, 0:TS])
            v_cp(srowb[0:1, 0:TS], srow[0:1, TS:2 * TS])
            t_mm(psF[:, 0:TS], ones_r[0:1, :], srowb[0:1, 0:TS])
            for q in range(PCH):
                v_tt(fm(kk_t, q, 0, TS), fm(kk_t, q, 0, TS), psF[:, 0:TS],
                     AL.mult)
            yield

            branch_lerp("a", 4, xs)
            for q in range(PCH):
                t_mm(psE[0:64, 0:TS], w_a1[:, q * 64:(q + 1) * 64],
                     fm(xs, q, 0, TS), start=(q == 0), stop=(q == PCH - 1))
            s_cp(hid64[0:64, 0:TS], psE[0:64, 0:TS])
            yield
            for m in range(PCH):
                t_mm(psF[:, 0:TS], w_a2[0:64, m * 128:(m + 1) * 128],
                     hid64[0:64, 0:TS], start=True, stop=True)
                s_act(fm(dsc, m, 0, TS), psF[:, 0:TS], ACT.Sigmoid,
                      bias=par["aaaaa"][:, m:m + 1])
            for q in range(PCH):
                v_stt(fm(Pb, q, 0, TS), fm(dsc, q, 0, TS), 2.0,
                      fm(kk_t, q, 0, TS), AL.mult, AL.mult)
            yield

            branch_lerp("g", 5, xs2)
            for q in range(PCH):
                t_mm(psE[0:128, 0:TS], w_g1[:, q * 128:(q + 1) * 128],
                     fm(xs2, q, 0, TS), start=(q == 0), stop=(q == PCH - 1))
            s_act(hid128[:, 0:TS], psE[:, 0:TS], ACT.Tanh)
            yield
            for mp in range(0, PCH, 2):
                for m2 in (mp, mp + 1):
                    co = (m2 & 1) * TS
                    t_mm(psF[:, co:co + TS], w_g2[:, m2 * 128:(m2 + 1) * 128],
                         hid128[:, 0:TS], start=(co == 0), stop=(co == TS))
                s_act(bass.AP(g_t, mp * TP1, [[FT1, 128], [TP1, 2], [1, TS]]),
                      psF[:, 0:2 * TS], ACT.Copy)
            dma_sp(bass.AP(gsp_h, tok0, [[TC, 128], [128 * TC, PCH], [1, TS]]),
                   bass.AP(g_t, 0, [[FT1, 128], [TP1, PCH], [1, TS]]))
            yield

            branch_lerp("v", 3, xs)

            def evac_v(m, ps):
                s_act(Pv[:, m * TS:(m + 2) * TS], ps[:, 0:2 * TS], ACT.Copy)

            big_proj(w_v, evac_v, xs)
            dma_sp(bass.AP(vsp_h, tok0, [[TC, 128], [128 * TC, PCH], [1, TS]]),
                   bass.AP(Pv, 0, [[FT, 128], [TS, PCH], [1, TS]]))
            yield

            branch_lerp("r", 0, xs2)

            def evac_r(m, ps):
                s_act(Pr[:, m * TS:(m + 2) * TS], ps[:, 0:2 * TS], ACT.Copy)

            big_proj(w_r, evac_r, xs2)
            yield

            for q in range(PCH):
                v_stt(fm(xs, q, 0, TS), fm(Pr, q, 0, TS),
                      par["faaaa"][:, q:q + 1], fm(Pk, q, 0, TS),
                      AL.mult, AL.mult)
            for q in range(PCH):
                t_mm(psE[0:16, 0:TS], hind[:, q * 16:(q + 1) * 16],
                     fm(xs, q, 0, TS), start=(q == 0), stop=(q == PCH - 1))
            s_cp(u_bf[0:16, tok0:tok0 + TS], psE[0:16, 0:TS])
            yield

            for q in range(PCH):
                v_stt(fm(Pks, q, 0, TS), fm(kk_t, q, 0, TS), -1.0,
                      fm1(epg, q, 0, TS), AL.mult, AL.mult)
            v_ts(bass.AP(Pks, 64, [[FT, 128], [TS, PCH], [64, CPT - 1]]),
                 bass.AP(kk_t, 64, [[FT, 128], [TS, PCH], [64, CPT - 1]]),
                 -1.0, None, AL.mult)
            for q in range(PCH):
                v_tt(fm(Pb, q, 0, TS), fm(Pb, q, 0, TS), fm1(en_t, q, 1, TS),
                     AL.mult)
            for q in range(PCH):
                v_tt(fm(Pk, q, 0, TS), fm(Pk, q, 0, TS),
                     fm1(en_t, q, 1, TS), AL.mult)
            for q in range(PCH):
                v_tt(fm(Pr, q, 0, TS), fm(Pr, q, 0, TS), fm1(epg, q, 1, TS),
                     AL.mult)

            if DEBUG:
                GE.wait(s_ve, s_ve.v)
                for nm, tt in (("d_r", Pr), ("d_kq", Pk), ("d_kk", kk_t),
                               ("d_b", Pb), ("d_af", Pks)):
                    dma_sp(bass.AP(dbg_h[nm], tok0,
                                   [[TC, 128], [128 * TC, PCH], [1, TS]]),
                           bass.AP(tt, 0, [[FT, 128], [TS, PCH], [1, TS]]))
                dma_sp(bass.AP(dbg_h["d_xln"], tok0,
                               [[TC, 128], [128 * TC, PCH], [1, TS]]),
                       bass.AP(xln, 1, [[FT1, 128], [TP1, PCH], [1, TS]]))
                dma_sp(bass.AP(dbg_h["d_ld"], tok0,
                               [[TC, 128], [128 * TC, PCH], [1, TS]]),
                       bass.AP(ldrel, 0, [[FT1, 128], [TP1, PCH], [1, TS]]))
                dma_sp(bass.AP(dbg_h["d_eng"], tok0,
                               [[TC, 128], [128 * TC, PCH], [1, TS]]),
                       bass.AP(en_t, 1, [[FT1, 128], [TP1, PCH], [1, TS]]))
            yield

        def scan_tile(ti):
            tok0 = ti * TS
            pa = ti % 2
            Pv, Pb, Pk, Pr, Pks = (v_tB[pa], b_tB[pa], kq_tB[pa], r_tB[pa],
                                   kscB[pa])
            ph("sweep1_t%d" % ti)
            for cc in range(CPT):
                c = ti * CPT + cc
                t0 = cc * L

                jobs = []
                for p in range(NP8):
                    for src, ps, key in ((Pv, psB, "tv"), (Pb, psC, "tb"),
                                         (Pk, psD, "tk")):
                        sl = fm(src, p, t0, L)
                        jobs.append((ps[EV, p * 64:p * 64 + 64], sl[EV, :],
                                     idn[EV, 0:64], TPE, (key, 0)))
                        jobs.append((ps[OD, p * 64:p * 64 + 64], sl[OD, :],
                                     idn[OD, 64:128], TPO, (key, 1)))
                mm_group(jobs)
                s_cp(vtm[:, :], psB[:, 0:512])
                s_cp(btm[:, :], psC[:, 0:512])
                s_cp(ktm[:, :], psD[:, 0:512])
                yield

                jobs = []
                for p in range(NP8):
                    afs = fm(Pks, p, t0, L)
                    bfs = fm(Pb, p, t0, L)
                    jobs.append((psA[EV, p * 64:p * 64 + 64], afs[EV, :],
                                 bfs[EV, :], TPE, ("A", 0)))
                    jobs.append((psA[OD, p * 64:p * 64 + 64], afs[OD, :],
                                 bfs[OD, :], TPO, ("A", 1)))
                mm_group(jobs)
                jobs = []
                for p in range(NP8):
                    ps, pc = (psB, p * 128) if p < 4 else (psC, (p - 4) * 128)
                    afs = fm(Pks, p, t0, L)
                    jobs.append((ps[EV, pc:pc + 128], afs[EV, :],
                                 zbf[EV, p * 128:(p + 1) * 128], TPE,
                                 ("R" + str(p // 4), 0)))
                    jobs.append((ps[OD, pc:pc + 128], afs[OD, :],
                                 zbf[OD, p * 128:(p + 1) * 128], TPO,
                                 ("R" + str(p // 4), 1)))
                firsts = {}
                for i, j in enumerate(jobs):
                    firsts.setdefault(j[4], i)
                for i, (o, lh, rh, tp, kkey) in enumerate(jobs):
                    t_mm(o, lh, rh, start=(firsts[kkey] == i), stop=False,
                         tp=tp)
                v_tt(m_sl[:, :], psA[:, 0:512],
                     bass.AP(msl, 0, [[64, 128], [0, NP8], [1, 64]]), AL.mult)
                yield

                jobs = []
                for p in range(NP8):
                    ps, pc = (psD, p * 128) if p < 4 else (psA, (p - 4) * 128)
                    bfs = fm(Pb, p, t0, L)
                    afs = fm(Pks, p, t0, L)
                    rfs = fm(Pr, p, t0, L)
                    jobs.append((ps[EV, pc:pc + 64], bfs[EV, :], afs[EV, :],
                                 TPE, ("D" + str(p // 4), 0)))
                    jobs.append((ps[OD, pc:pc + 64], bfs[OD, :], afs[OD, :],
                                 TPO, ("D" + str(p // 4), 1)))
                    jobs.append((ps[EV, pc + 64:pc + 128], bfs[EV, :],
                                 rfs[EV, :], TPE, ("D" + str(p // 4), 0)))
                    jobs.append((ps[OD, pc + 64:pc + 128], bfs[OD, :],
                                 rfs[OD, :], TPO, ("D" + str(p // 4), 1)))
                mm_group(jobs)
                for half, ps in ((0, psD), (1, psA)):
                    v_tt(m_su[:, half * 256:half * 256 + 256],
                         bass.AP(ps, 0, [[512, 128], [128, 4], [1, 64]]),
                         bass.AP(msu, 0, [[128, 128], [0, 4], [1, 64]]),
                         AL.mult)
                    v_tt(lrb[:, half * 256:half * 256 + 256],
                         bass.AP(ps, 64, [[512, 128], [128, 4], [1, 64]]),
                         bass.AP(msu, 64, [[128, 128], [0, 4], [1, 64]]),
                         AL.mult)
                yield

                jobs = []
                for p in range(NP8):
                    ps, pc = (psG, p * 128) if p < 4 else (psH, (p - 4) * 128)
                    kfs = fm(Pk, p, t0, L)
                    afs = fm(Pks, p, t0, L)
                    rfs = fm(Pr, p, t0, L)
                    jobs.append((ps[EV, pc:pc + 64], kfs[EV, :], afs[EV, :],
                                 TPE, ("D" + str(p // 4), 0)))
                    jobs.append((ps[OD, pc:pc + 64], kfs[OD, :], afs[OD, :],
                                 TPO, ("D" + str(p // 4), 1)))
                    jobs.append((ps[EV, pc + 64:pc + 128], kfs[EV, :],
                                 rfs[EV, :], TPE, ("D" + str(p // 4), 0)))
                    jobs.append((ps[OD, pc + 64:pc + 128], kfs[OD, :],
                                 rfs[OD, :], TPO, ("D" + str(p // 4), 1)))
                mm_group(jobs)
                for half, ps in ((0, psG), (1, psH)):
                    v_tt(t1l[:, half * 256:half * 256 + 256],
                         bass.AP(ps, 0, [[512, 128], [128, 4], [1, 64]]),
                         bass.AP(msu, 0, [[128, 128], [0, 4], [1, 64]]),
                         AL.mult)
                    v_tt(lrk[:, half * 256:half * 256 + 256],
                         bass.AP(ps, 64, [[512, 128], [128, 4], [1, 64]]),
                         bass.AP(msu, 64, [[128, 128], [0, 4], [1, 64]]),
                         AL.mult)
                yield

                jobs = []
                for p in range(NP8):
                    ps, pc = (psB, p * 128) if p < 4 else (psC, (p - 4) * 128)
                    jobs.append((ps[EV, pc + 64:pc + 128],
                                 t1l[EV, p * 64:p * 64 + 64],
                                 vtm[EV, p * 64:p * 64 + 64], TPE,
                                 ("T" + str(p // 4), 0)))
                    jobs.append((ps[OD, pc + 64:pc + 128],
                                 t1l[OD, p * 64:p * 64 + 64],
                                 vtm[OD, p * 64:p * 64 + 64], TPO,
                                 ("T" + str(p // 4), 1)))
                lastk = {}
                for i, j in enumerate(jobs):
                    lastk[j[4]] = i
                for i, (o, lh, rh, tp, kkey) in enumerate(jobs):
                    t_mm(o, lh, rh, start=False, stop=(lastk[kkey] == i), tp=tp)
                s_cp(xa_t[:, 0:512], psB[:, 0:512])
                s_cp(xa_t[:, 512:1024], psC[:, 0:512])
                yield

                xcur, xnxt = xa_t, xb_t
                pcur, ncur, pnxt, nnxt = pj_a, nj_a, pj_b, nj_b
                for j in range(NSOLVE):
                    pref = m_sl if j == 0 else pcur
                    nref = m_su if j == 0 else ncur
                    for h in (0, 1):
                        ps = psB if h == 0 else psC
                        jobs = []
                        for p in range(h * 4, h * 4 + 4):
                            pc = (p - h * 4) * 128
                            xsl = xcur[:, p * 128:(p + 1) * 128]
                            jobs.append((ps[EV, pc:pc + 128],
                                         nref[EV, p * 64:p * 64 + 64],
                                         xsl[EV, :], TPE, ("X", h, 0)))
                            jobs.append((ps[OD, pc:pc + 128],
                                         nref[OD, p * 64:p * 64 + 64],
                                         xsl[OD, :], TPO, ("X", h, 1)))
                        if j < NSOLVE - 1:
                            for p in range(h * 4, h * 4 + 4):
                                co = p * 64
                                jobs.append((psD[EV, co:co + 64],
                                             nref[EV, co:co + 64],
                                             pref[EV, co:co + 64], TPE,
                                             ("P", h, 0)))
                                jobs.append((psD[OD, co:co + 64],
                                             nref[OD, co:co + 64],
                                             pref[OD, co:co + 64], TPO,
                                             ("P", h, 1)))
                                jobs.append((psA[EV, co:co + 64],
                                             pref[EV, co:co + 64],
                                             nref[EV, co:co + 64], TPE,
                                             ("N", h, 0)))
                                jobs.append((psA[OD, co:co + 64],
                                             pref[OD, co:co + 64],
                                             nref[OD, co:co + 64], TPO,
                                             ("N", h, 1)))
                        mm_group(jobs)
                        v_tt(xnxt[:, h * 512:(h + 1) * 512], ps[:, 0:512],
                             xcur[:, h * 512:(h + 1) * 512], AL.add)
                        if j < NSOLVE - 1:
                            s_cp(pnxt[:, h * 256:(h + 1) * 256],
                                 psD[:, h * 256:(h + 1) * 256])
                            s_cp(nnxt[:, h * 256:(h + 1) * 256],
                                 psA[:, h * 256:(h + 1) * 256])
                    if j < NSOLVE - 1:
                        pcur, pnxt = pnxt, pcur
                        ncur, nnxt = nnxt, ncur
                    xcur, xnxt = xnxt, xcur
                    if j % 2 == 1:
                        yield
                xfin = xcur

                for h in (0, 1):
                    jobs = []
                    for p in range(h * 4, h * 4 + 4):
                        zD = zbf[:, p * 128 + 64:(p + 1) * 128]
                        zP = zbf[:, p * 128:p * 128 + 64]
                        rfs = fm(Pr, p, t0, L)
                        co = p * 64
                        jobs.append((psD[EV, co:co + 64], zD[EV, :],
                                     rfs[EV, :], TPE, ("Y", h, 0)))
                        jobs.append((psD[OD, co:co + 64], zD[OD, :],
                                     rfs[OD, :], TPO, ("Y", h, 1)))
                        jobs.append((psA[EV, co:co + 64], zP[EV, :],
                                     rfs[EV, :], TPE, ("C", h, 0)))
                        jobs.append((psA[OD, co:co + 64], zP[OD, :],
                                     rfs[OD, :], TPO, ("C", h, 1)))
                    for p in range(h * 4, h * 4 + 4):
                        xD = xfin[:, p * 128 + 64:(p + 1) * 128]
                        xP = xfin[:, p * 128:p * 128 + 64]
                        co = p * 64
                        jobs.append((psD[EV, co:co + 64], xD[EV, :],
                                     lrb[EV, co:co + 64], TPE, ("Y", h, 0)))
                        jobs.append((psD[OD, co:co + 64], xD[OD, :],
                                     lrb[OD, co:co + 64], TPO, ("Y", h, 1)))
                        jobs.append((psA[EV, co:co + 64], xP[EV, :],
                                     lrb[EV, co:co + 64], TPE, ("C", h, 0)))
                        jobs.append((psA[OD, co:co + 64], xP[OD, :],
                                     lrb[OD, co:co + 64], TPO, ("C", h, 1)))
                    for p in range(h * 4, h * 4 + 4):
                        co = p * 64
                        jobs.append((psD[EV, co:co + 64], vtm[EV, co:co + 64],
                                     lrk[EV, co:co + 64], TPE, ("Y", h, 0)))
                        jobs.append((psD[OD, co:co + 64], vtm[OD, co:co + 64],
                                     lrk[OD, co:co + 64], TPO, ("Y", h, 1)))
                    mm_group(jobs)
                    s_cp(xb_t[:, 512 + h * 256:512 + (h + 1) * 256],
                         psD[:, h * 256:(h + 1) * 256])
                    s_cp(xb_t[:, h * 256:(h + 1) * 256],
                         psA[:, h * 256:(h + 1) * 256])
                dma_begin()
                dma_sp(bass.AP(ylsp_h, tok0 + t0,
                               [[TC, 128], [128 * TC, PCH], [1, L]]),
                       xb_t[:, 512:1024])
                dma_sp(bass.AP(ycsp_h, tok0 + t0,
                               [[NP8 * TC, 128], [TC, NP8], [1, L]]),
                       xb_t[:, 0:512])
                dma_end()
                yield

                jobs = []
                for p in range(NP8):
                    ps, pc = (psB, p * 128) if p < 4 else (psC, (p - 4) * 128)
                    zsl = zbf[:, p * 128:(p + 1) * 128]
                    xsl = xfin[:, p * 128:(p + 1) * 128]
                    jobs.append((ps[EV, pc:pc + 128], idn[EV, 0:64],
                                 zsl[EV, :], TPE, ("Z" + str(p // 4), 0)))
                    jobs.append((ps[OD, pc:pc + 128], idn[OD, 64:128],
                                 zsl[OD, :], TPO, ("Z" + str(p // 4), 1)))
                    jobs.append((ps[EV, pc:pc + 128],
                                 btm[EV, p * 64:p * 64 + 64], xsl[EV, :], TPE,
                                 ("Z" + str(p // 4), 0)))
                    jobs.append((ps[OD, pc:pc + 128],
                                 btm[OD, p * 64:p * 64 + 64], xsl[OD, :], TPO,
                                 ("Z" + str(p // 4), 1)))
                    jobs.append((ps[EV, pc + 64:pc + 128],
                                 ktm[EV, p * 64:p * 64 + 64],
                                 vtm[EV, p * 64:p * 64 + 64], TPE,
                                 ("Z" + str(p // 4), 0)))
                    jobs.append((ps[OD, pc + 64:pc + 128],
                                 ktm[OD, p * 64:p * 64 + 64],
                                 vtm[OD, p * 64:p * 64 + 64], TPO,
                                 ("Z" + str(p // 4), 1)))
                mm_group(jobs)
                for p in range(NP8):
                    ps, pc = (psB, p * 128) if p < 4 else (psC, (p - 4) * 128)
                    v_ts(z32[:, p * 128:(p + 1) * 128], ps[:, pc:pc + 128],
                         d63[:, p * NCH + c:p * NCH + c + 1], None, AL.mult)
                s_cp(zbf[:, :], z32[:, :])
                yield

        def sweep2_loads(ti):
            tok0 = ti * TS
            pa = ti % 2
            Syl = (b_t, b_t2)[pa]
            Syc = (ksc, ksc2)[pa]
            Sv = (v_t, v_t2)[pa]
            Sg = (g_t, x2bf)[pa]

            def xg(m):
                if pa == 0:
                    return fm1(x32, m, 0, TS)
                wb = wstg if m < 4 else wstg2
                return wb[:, (m % 4) * TS:(m % 4 + 1) * TS]

            dma_begin()
            dma_in(bass.AP(Syl, 0, [[FT, 128], [TS, PCH], [1, TS]]),
                   bass.AP(ylsp_h, tok0,
                           [[TC, 128], [128 * TC, PCH], [1, TS]]))
            dma_in(bass.AP(Syc, 0, [[FT, 128], [TS, NP8], [1, TS]]),
                   bass.AP(ycsp_h, tok0,
                           [[NP8 * TC, 128], [TC, NP8], [1, TS]]))
            dma_sp(bass.AP(Sv, 0, [[FT, 128], [TS, PCH], [1, TS]]),
                   bass.AP(vsp_h, tok0,
                           [[TC, 128], [128 * TC, PCH], [1, TS]]))
            dma_sp(bass.AP(Sg, 0, [[FT1, 128], [TP1, PCH], [1, TS]]),
                   bass.AP(gsp_h, tok0,
                           [[TC, 128], [128 * TC, PCH], [1, TS]]))
            dma_end()


        def tailprep_gen():
            for _ in ffnprep_gen():
                yield

        def ffnprep_gen():
            big_weight(w_r, W_h["W_o"])
            yield
            for q in range(PCH):
                dma_in(wstg[:, :],
                       W_h["ffn_w"].ap()[q * 128:(q + 1) * 128, :])
                v_red(gmax[:, NT + 1 + q:NT + 2 + q], wstg[:, :], AL.add,
                      absv=True)
                yield
            v_red(gcol[:, 0:1], gmax[:, NT + 1:NT + 9], AL.add)
            t_mm(psF[0:1, 0:128], gcol[:, 0:1], idn32[:, :])
            v_red(srow[0:1, 0:1], psF[0:1, 0:128], AL.add)
            t_mm(psE[:, 0:1], ones_r32[0:1, :], srow[0:1, 0:1])
            v_cp(gcol[:, 1:2], psE[:, 0:1])
            v_ts(gcol[:, 1:2], gcol[:, 1:2], 1.0 / (C * C), None, AL.mult)
            v_ts(gcol[:, 2:3], gcol[:, 1:2], FFN_EPS, None, AL.add)
            v_recip(gcol[:, 3:4], gcol[:, 2:3])
            if DEBUG:
                GE.wait(s_ve, s_ve.v)
                dma_sp(dbg_h["d_gc1"].ap(), gcol[:, 0:8])
            yield

        def quant_gen():
            for q in range(PCH):
                dma_in(wstg[:, :],
                       W_h["ffn_w"].ap()[q * 128:(q + 1) * 128, :])
                v_ts(wstg[:, :], wstg[:, :], gcol[:, 3:4], MAGIC, AL.mult,
                     AL.add)
                v_ts(wstg[:, :], wstg[:, :], -MAGIC, None, AL.add)
                v_ts(w_k[:, q * C:(q + 1) * C], wstg[:, :], 1.0, -1.0, AL.min,
                     AL.max)
                yield
            if DEBUG:
                GE.wait(s_ve, s_ve.v)
                dma_sp(dbg_h["d_gc2"].ap(), gcol[:, 0:8])

        _END = object()

        def drain(g, n):
            for _ in range(n):
                if next(g, _END) is _END:
                    return False
            return True

        gw = weights_gen()
        gp0 = prep_tile(0)
        aw = ap0 = True
        while aw or ap0:
            if aw:
                aw = drain(gw, 3)
            if ap0:
                ap0 = drain(gp0, 1)
        for ti in range(NT):
            gs = scan_tile(ti)
            gp = prep_tile(ti + 1) if ti + 1 < NT else tailprep_gen()
            a1 = a2 = True
            while a1 or a2:
                if a1:
                    a1 = drain(gs, 2)
                if a2:
                    a2 = drain(gp, 1)

        ph("scan_done")
        for t in range(min(2, NT)):
            sweep2_loads(t)

        def ag1_gen():
            TE.wait(s_sc, s_sc.v)
            TE.wait(s_ve, s_ve.v)
            jobs = []
            for p in range(NP8):
                ps, pc = (psB, p * 128) if p < 4 else (psC, (p - 4) * 128)
                jobs.append((ps[EV, pc:pc + 64], z32[EV, p * 128:p * 128 + 64],
                             idn32[EV, 0:64], TPE, ("zt" + str(p // 4), 0)))
                jobs.append((ps[OD, pc:pc + 64], z32[OD, p * 128:p * 128 + 64],
                             idn32[OD, 64:128], TPO, ("zt" + str(p // 4), 1)))
            mm_group(jobs)
            mm_zt = TE.chk(s_mm)
            SE.wait(s_mm, mm_zt)
            for p in range(NP8):
                ps, pc = (psB, p * 128) if p < 4 else (psC, (p - 4) * 128)
                s_cp(agst[:, p * 128:p * 128 + 64], ps[:, pc:pc + 64])
                s_cp(agst[:, p * 128 + 64:(p + 1) * 128],
                     z32[:, p * 128 + 64:(p + 1) * 128])
            sc_ag = SE.chk(s_sc)
            GE.wait(s_sc, sc_ag)
            dma_sp(ag1_in.ap(), agst[:, 0:1024])
            yield
            GE.wait(s_sp, s_sp.v)
            cc1 = E(GE, lambda e: e.collective_compute(
                "AllGather", AL.bypass,
                replica_groups=[list(range(NCOR))],
                ins=[ag1_in.ap().opt()],
                outs=[ag1_out.ap().opt()],
            ), rd=[ag1_in.ap()], wr=[ag1_out.ap()], inc=1, asyn=True)
            v_set(z32[:, 0:512], 0.0)
            v_set(xa_t[:, 0:512], 0.0)
            GE.wait(s_cc, cc1)
            yield
            dma_sp(agrm[:, 0:1024], ag1_out.ap()[0])
            for jc in range(NCOR - 1):
                rb = (jc & 1) * 1024
                if jc + 1 < NCOR - 1:
                    nb = ((jc + 1) & 1) * 1024
                    dma_sp(agrm[:, nb:nb + 1024], ag1_out.ap()[jc + 1])
                s_cp(agrb[:, rb:rb + 1024], agrm[:, rb:rb + 1024])
                jobs = []
                for p in range(NP8):
                    jobs.append((psA[EV, p * 64:p * 64 + 64],
                                 agrb[EV, rb + p * 128:rb + p * 128 + 64],
                                 xa_t[EV, p * 64:p * 64 + 64], TPE, ("G", 0)))
                    jobs.append((psA[OD, p * 64:p * 64 + 64],
                                 agrb[OD, rb + p * 128:rb + p * 128 + 64],
                                 xa_t[OD, p * 64:p * 64 + 64], TPO, ("G", 1)))
                mm_group(jobs)
                v_tt(x32[:, 0:512], psA[:, 0:512],
                     bass.AP(agrb, rb + 64, [[FT, 128], [128, NP8], [1, 64]]),
                     AL.add)
                v_ts(x32[:, 0:512], x32[:, 0:512], sel128[:, jc:jc + 1], None, AL.mult)
                v_stt(x32[:, 512:1024], z32[:, 0:512], sel128[:, jc:jc + 1], z32[:, 0:512],
                      AL.mult, AL.subtract)
                v_tt(z32[:, 0:512], x32[:, 0:512], x32[:, 512:1024], AL.subtract)
                v_cp(xa_t[:, 0:512], z32[:, 0:512])
                yield

            if DEBUG:
                GE.wait(s_ve, s_ve.v)
                dma_sp(dbg_h["d_zin"].ap(), z32[:, 0:512])


        g1g = ag1_gen()
        g2g = quant_gen()
        a1 = a2 = True
        while a1 or a2:
            if a1:
                a1 = drain(g1g, 1)
            if a2:
                a2 = drain(g2g, 1)

        ph("ag1")

        ph("ffn_prep")

        def sweep2_tile(ti, preloaded=False):
            tok0 = ti * TS
            pa = ti % 2
            Syl = (b_t, b_t2)[pa]
            Syc = (ksc, ksc2)[pa]
            Sv = (v_t, v_t2)[pa]
            Sg = (g_t, x2bf)[pa]
            Sy = (kk_t, r_t2)[pa]
            Ssq = (dsc, xx)[pa]
            Sxn = (kq_t, kq_t2)[pa]
            Sxb = (xs, r_t)[pa]
            gb0 = pa * 1024
            So = (lg_t, ldg)[pa]
            B1, B2, B3 = ((psA, psB, psC), (psD, psE, psF))[pa]

            def xg(m):
                if pa == 0:
                    return fm1(x32, m, 0, TS)
                wb = wstg if m < 4 else wstg2
                return wb[:, (m % 4) * TS:(m % 4 + 1) * TS]

            if not preloaded:
                sweep2_loads(ti)
            if pa == 0:
                dma_in(bass.AP(x32, 0, [[FT1, 128], [TP1, PCH], [1, TS]]),
                       bass.AP(xT_h, tok0 + 1,
                               [[TC + 1, 128], [128 * (TC + 1), PCH],
                                [1, TS]]))
            else:
                dma_begin()
                for hb, wb in ((0, wstg), (1, wstg2)):
                    dma_in(bass.AP(wb, 0, [[C, 128], [TS, 4], [1, TS]]),
                           bass.AP(xT_h, tok0 + 1 + hb * 4 * 128 * (TC + 1),
                                   [[TC + 1, 128], [128 * (TC + 1), 4],
                                    [1, TS]]))
                dma_end()
            yield

            for cc in range(CPT):
                t0 = cc * L
                jobs = []
                for p in range(NP8):
                    jobs.append((B1[EV, p * 64:p * 64 + 64],
                                 xa_t[EV, p * 64:p * 64 + 64],
                                 Syc[EV, p * TS + t0:p * TS + t0 + L], TPE,
                                 ("c", 0)))
                    jobs.append((B1[OD, p * 64:p * 64 + 64],
                                 xa_t[OD, p * 64:p * 64 + 64],
                                 Syc[OD, p * TS + t0:p * TS + t0 + L], TPO,
                                 ("c", 1)))
                mm_group(jobs)
                v_tt(bass.AP(Sy, t0, [[FT, 128], [TS, NP8], [1, L]]),
                     bass.AP(B1, 0, [[512, 128], [64, NP8], [1, L]]),
                     bass.AP(Syl, t0, [[FT, 128], [TS, NP8], [1, L]]),
                     AL.add)
                if cc % 2 == 1:
                    yield

            if DEBUG:
                GE.wait(s_ve, s_ve.v)
                dma_sp(bass.AP(dbg_h["d_ycr"], tok0,
                               [[TC, 128], [128 * TC, PCH], [1, TS]]),
                       bass.AP(Sy, 0, [[FT, 128], [TS, PCH], [1, TS]]))
            for q in range(PCH):
                s_act(fm(Ssq, q, 0, TS), fm(Sy, q, 0, TS), ACT.Square)
            for q in range(PCH):
                t_mm(B2[0:16, 0:TS], hind[:, q * 16:(q + 1) * 16],
                     fm(Sy, q, 0, TS), start=(q == 0), stop=(q == PCH - 1))
            for q in range(PCH):
                t_mm(B3[0:16, 0:TS], hind[:, q * 16:(q + 1) * 16],
                     fm(Ssq, q, 0, TS), start=(q == 0), stop=(q == PCH - 1))
            yield
            v_ts(gnr[0:16, 0:TS], B2[0:16, 0:TS], 1.0 / 64, None, AL.mult)
            v_ts(gnr[0:16, TS:2 * TS], B3[0:16, 0:TS], 1.0 / 64, None,
                 AL.mult)
            s_act(gnr[0:16, 2 * TS:3 * TS], gnr[0:16, 0:TS], ACT.Square)
            v_tt(gnr[0:16, TS:2 * TS], gnr[0:16, TS:2 * TS],
                 gnr[0:16, 2 * TS:3 * TS], AL.subtract)
            s_act(gnr[0:16, 2 * TS:3 * TS], gnr[0:16, TS:2 * TS], ACT.Sqrt,
                  bias=GN_EPS)
            v_recip(gnr[0:16, 3 * TS:4 * TS], gnr[0:16, 2 * TS:3 * TS])
            v_tt(gnr[0:16, 0:TS], gnr[0:16, 0:TS], gnr[0:16, 3 * TS:4 * TS],
                 AL.mult)
            v_ts(gnr[0:16, 0:TS], gnr[0:16, 0:TS], -1.0, None, AL.mult)
            v_cp(xln[0:16, gb0:gb0 + TS], gnr[0:16, 3 * TS:4 * TS])
            v_cp(xln[0:16, gb0 + TS:gb0 + 2 * TS], gnr[0:16, 0:TS])
            yield

            for q in range(PCH):
                t_mm(B2[:, 0:TS], hexp[0:16, q * 128:(q + 1) * 128],
                     xln[0:16, gb0:gb0 + TS], start=True, stop=True)
                t_mm(B3[:, 0:TS], hexp[0:16, q * 128:(q + 1) * 128],
                     xln[0:16, gb0 + TS:gb0 + 2 * TS], start=True, stop=True)
                t_mm(B1[:, 0:TS], hexp[0:16, q * 128:(q + 1) * 128],
                     u_bf[0:16, tok0:tok0 + TS], start=True, stop=True)
                v_tt(fm(Sy, q, 0, TS), fm(Sy, q, 0, TS), B2[:, 0:TS],
                     AL.mult)
                v_tt(fm(Sy, q, 0, TS), fm(Sy, q, 0, TS), B3[:, 0:TS],
                     AL.add)
                v_ts(fm(Sy, q, 0, TS), fm(Sy, q, 0, TS),
                     par["ln_w"][:, q:q + 1], par["ln_b"][:, q:q + 1],
                     AL.mult, AL.add)
                v_tt(fm(Sxb, q, 0, TS), B1[:, 0:TS], fm(Sv, q, 0, TS),
                     AL.mult)
                v_tt(fm(Sy, q, 0, TS), fm(Sy, q, 0, TS), fm(Sxb, q, 0, TS),
                     AL.add)
                v_tt(fm(Sy, q, 0, TS), fm(Sy, q, 0, TS),
                     fm(Sg, q, 0, TS, W=TP1), AL.mult)
                if q % 2 == 1:
                    yield

            if DEBUG:
                GE.wait(s_ve, s_ve.v)
                dma_sp(bass.AP(dbg_h["d_yfin"], tok0,
                               [[TC, 128], [128 * TC, PCH], [1, TS]]),
                       bass.AP(Sy, 0, [[FT, 128], [TS, PCH], [1, TS]]))
            for mp in range(0, PCH, 2):
                ps = B2 if (mp % 4 == 0) else B3
                for m2 in (mp, mp + 1):
                    co = (m2 & 1) * TS
                    for q in range(PCH):
                        t_mm(ps[:, co:co + TS],
                             w_r[:, q * C + m2 * 128: q * C + (m2 + 1) * 128],
                             fm(Sy, q, 0, TS),
                             start=(q == 0 and co == 0),
                             stop=(q == PCH - 1 and co == TS))
                if pa == 0:
                    xpair = bass.AP(x32, mp * TP1,
                                    [[FT1, 128], [TP1, 2], [1, TS]])
                else:
                    wb = wstg if mp < 4 else wstg2
                    xpair = wb[:, (mp % 4) * TS:(mp % 4) * TS + 2 * TS]
                v_tt(So[:, mp * TS:(mp + 2) * TS], ps[:, 0:2 * TS], xpair,
                     AL.add)
                yield
            dma_sp(bass.AP(x1sp_h, tok0, [[TC, 128], [128 * TC, PCH], [1, TS]]),
                   bass.AP(So, 0, [[FT if pa == 0 else FT1, 128], [TS, PCH],
                                   [1, TS]]))
            if DEBUG:
                dma_sp(bass.AP(dbg_h["d_ot"], tok0,
                               [[TC, 128], [128 * TC, PCH], [1, TS]]),
                       bass.AP(So, 0, [[FT if pa == 0 else FT1, 128],
                                       [TS, PCH], [1, TS]]))

            s_cp(Ssq[:, 0:FT], So[:, 0:FT])
            s_act(Sxn[:, 0:FT], So[:, 0:FT], ACT.Square)
            for q in range(PCH):
                t_mm(B1[0:1, 0:TS], ones_c[:, 0:1], fm(Ssq, q, 0, TS),
                     start=(q == 0), stop=(q == PCH - 1))
            for q in range(PCH):
                t_mm(B2[0:1, 0:TS], ones_c[:, 0:1], fm(Sxn, q, 0, TS),
                     start=(q == 0), stop=(q == PCH - 1))
            yield
            v_ts(srow[0:1, 0:TS], B1[0:1, 0:TS], 1.0 / C, None, AL.mult)
            v_ts(srow[0:1, TP1:TP1 + TS], B2[0:1, 0:TS], 1.0 / C, None,
                 AL.mult)
            s_act(srow2[0:1, 0:TS], srow[0:1, 0:TS], ACT.Square)
            v_tt(srow2[0:1, TP1:TP1 + TS], srow[0:1, TP1:TP1 + TS],
                 srow2[0:1, 0:TS], AL.subtract)
            s_act(srow2[0:1, 2 * TP1:2 * TP1 + TS],
                  srow2[0:1, TP1:TP1 + TS], ACT.Sqrt, bias=LN2_EPS)
            v_recip(srow[0:1, 2 * TP1:2 * TP1 + TS],
                    srow2[0:1, 2 * TP1:2 * TP1 + TS])
            v_tt(srow2[0:1, 0:TS], srow[0:1, 0:TS],
                 srow[0:1, 2 * TP1:2 * TP1 + TS], AL.mult)
            v_ts(srow2[0:1, 0:TS], srow2[0:1, 0:TS], -1.0, None, AL.mult)
            v_cp(srowb[0:1, 0:TS], srow[0:1, 2 * TP1:2 * TP1 + TS])
            v_cp(srowb[0:1, TP1:TP1 + TS], srow2[0:1, 0:TS])
            t_mm(B3[:, 0:TS], ones_r[0:1, :], srowb[0:1, 0:TS])
            t_mm(B1[:, 0:TS], ones_r[0:1, :], srowb[0:1, TP1:TP1 + TS])
            yield
            for q in range(PCH):
                v_tt(fm(Sxn, q, 0, TS), fm(Ssq, q, 0, TS), B3[:, 0:TS],
                     AL.mult)
                v_tt(fm(Sxn, q, 0, TS), fm(Sxn, q, 0, TS), B1[:, 0:TS],
                     AL.add)
                v_ts(fm(Sxn, q, 0, TS), fm(Sxn, q, 0, TS),
                     par["ffn_g"][:, q:q + 1], par["ffn_b"][:, q:q + 1],
                     AL.mult, AL.add)
            dma_sp(bass.AP(xnsp_h, tok0, [[TC, 128], [128 * TC, PCH], [1, TS]]),
                   bass.AP(Sxn, 0, [[FT, 128], [TS, PCH], [1, TS]]))
            v_red(gmax[:, ti:ti + 1], Sxn[:, 0:FT], AL.max, absv=True)
            yield

        for tp in range(0, NT, 2):
            g1 = sweep2_tile(tp, preloaded=(tp == 0))
            g2 = (sweep2_tile(tp + 1, preloaded=(tp == 0 and NT > 1))
                  if tp + 1 < NT else None)
            a1, a2 = True, g2 is not None
            if a1:
                a1 = drain(g1, 4)
            while a1 or a2:
                if a1:
                    a1 = drain(g1, 1)
                if a2:
                    a2 = drain(g2, 1)

        def sweep3_loads(ti):
            tok0 = ti * TS
            pa = ti % 2
            Sxn = (kq_t, kq_t2)[pa]
            So = (lg_t, ldg)[pa]
            dma_begin()
            dma_in(bass.AP(Sxn, 0, [[FT, 128], [TS, PCH], [1, TS]]),
                   bass.AP(xnsp_h, tok0,
                           [[TC, 128], [128 * TC, PCH], [1, TS]]))
            dma_sp(bass.AP(So, 0, [[FT if pa == 0 else FT1, 128], [TS, PCH],
                                   [1, TS]]),
                   bass.AP(x1sp_h, tok0,
                           [[TC, 128], [128 * TC, PCH], [1, TS]]))
            dma_end()

        def sweep3_tile(ti, preloaded=False):
            tok0 = ti * TS
            pa = ti % 2
            Sxn = (kq_t, kq_t2)[pa]
            So = (lg_t, ldg)[pa]
            Sxq = (xs, r_t)[pa]
            Ba, Bb = ((psE, psF), (psB, psC))[pa]
            if not preloaded:
                sweep3_loads(ti)
            yield
            for q in range(PCH):
                v_ts(fm(Sxq, q, 0, TS), fm(Sxn, q, 0, TS), gcol[:, 7:8],
                     QB - 1e-5, AL.mult, AL.min)
                v_ts(fm(Sxq, q, 0, TS), fm(Sxq, q, 0, TS), -QB + 1e-5, None,
                     AL.max)
            if DEBUG and ti == 0:
                GE.wait(s_ve, s_ve.v)
                dma_sp(dbg_h["d_xq"].ap(),
                       bass.AP(Sxq, 0, [[FT, 128], [1, FT]]))
            yield
            for mp in range(0, PCH, 2):
                ps = Ba if (mp % 4 == 0) else Bb
                for m2 in (mp, mp + 1):
                    co = (m2 & 1) * TS
                    for q in range(PCH):
                        t_mm(ps[:, co:co + TS],
                             w_k[:, q * C + m2 * 128: q * C + (m2 + 1) * 128],
                             fm(Sxq, q, 0, TS),
                             start=(q == 0 and co == 0),
                             stop=(q == PCH - 1 and co == TS))
                v_stt(So[:, mp * TS:(mp + 2) * TS], ps[:, 0:2 * TS],
                      gcol[:, 5:6], So[:, mp * TS:(mp + 2) * TS],
                      AL.mult, AL.add)
                yield
            dma_sp(bass.AP(out_h, tok0, [[TC, 128], [128 * TC, PCH], [1, TS]]),
                   bass.AP(So, 0, [[FT if pa == 0 else FT1, 128], [TS, PCH],
                                   [1, TS]]))
            yield


        ph("sweep2")
        v_red(gcol[:, 4:5], gmax[:, 0:NT], AL.max)
        ve_gm = VE.chk(s_ve)
        if DEBUG:
            GE.wait(s_ve, ve_gm)
            dma_sp(dbg_h["d_gc3"].ap(), gcol[:, 0:8])
        GE.wait(s_ve, ve_gm)
        dma_sp(ag2_in.ap(), gcol[:, 4:5])
        GE.wait(s_sp, s_sp.v)
        cc2 = E(GE, lambda e: e.collective_compute(
            "AllGather", AL.bypass,
            replica_groups=[list(range(NCOR))],
            ins=[ag2_in.ap().opt()],
            outs=[ag2_out.ap().opt()],
        ), rd=[ag2_in.ap()], wr=[ag2_out.ap()], inc=1, asyn=True)
        GE.wait(s_cc, cc2)
        for t in range(min(2, NT)):
            sweep3_loads(t)
        gth = dma_sp(bass.AP(gmax, NT + 1, [[NT + 10, 128], [1, 8]]),
                     bass.AP(ag2_out, 0, [[8, 128], [1, 8]]))
        VE.wait(s_sp, gth)
        v_red(gcol[:, 5:6], gmax[:, NT + 1:NT + 9], AL.max)
        ve_g8 = VE.chk(s_ve)
        if DEBUG:
            GE.wait(s_ve, ve_g8)
            dma_sp(dbg_h["d_gmx"].ap(), gmax[:, 0:NT + 10])
        TE.wait(s_ve, ve_g8)
        t_mm(psA[0:1, 0:128], gcol[:, 5:6], idn32[:, :])
        mm_gt = TE.chk(s_mm)
        VE.wait(s_mm, mm_gt)
        v_red(srow[0:1, 0:1], psA[0:1, 0:128], AL.max)
        ve_gt = VE.chk(s_ve)
        TE.wait(s_ve, ve_gt)
        t_mm(psB[:, 0:1], ones_r32[0:1, :], srow[0:1, 0:1])
        mm_gb = TE.chk(s_mm)
        VE.wait(s_mm, mm_gb)
        v_cp(gcol[:, 6:7], psB[:, 0:1])
        v_recip(gcol[:, 7:8], gcol[:, 6:7])
        v_ts(gcol[:, 7:8], gcol[:, 7:8], QB, None, AL.mult)
        v_tt(gcol[:, 5:6], gcol[:, 6:7], gcol[:, 1:2], AL.mult)
        v_ts(gcol[:, 5:6], gcol[:, 5:6], 1.0 / QB, None, AL.mult)
        ve_sc = VE.chk(s_ve)
        if DEBUG:
            GE.wait(s_ve, ve_sc)
            dma_sp(dbg_h["d_gcol"].ap(), gcol[:, 0:8])
            dma_sp(dbg_h["d_wq"].ap(), w_k[:, 0:C])

        ph("ag2")

        for tp in range(0, NT, 2):
            g1 = sweep3_tile(tp, preloaded=(tp == 0))
            g2 = (sweep3_tile(tp + 1, preloaded=(tp == 0 and NT > 1))
                  if tp + 1 < NT else None)
            a1, a2 = True, g2 is not None
            while a1 or a2:
                if a1:
                    a1 = drain(g1, 1)
                if a2:
                    a2 = drain(g2, 1)
        GE.wait(s_st, s_st.v)

        ph("sweep3")
        with nc.Block() as block:
            @block.tensor
            def _(e):
                TE.run(e)

            @block.vector
            def _(e):
                VE.run(e)

            @block.scalar
            def _(e):
                SE.run(e)

            @block.gpsimd
            def _(e):
                GE.run(e)

            @block.sync
            def _(e):
                SY.run(e)

    return nc


_PROG_CACHE = {}


def _host_inputs(inputs):
    x = np.ascontiguousarray(np.asarray(inputs["L0_attn_x"],
                                        dtype=np.float32).reshape(T_ALL, C))
    xT = np.ascontiguousarray(x.T)
    xTp = np.concatenate([np.zeros((C, 1), np.float32), xT], axis=1)

    def pk(v):
        return np.ascontiguousarray(
            np.asarray(v, dtype=np.float32).reshape(-1).reshape(PCH, 128).T)

    par = {
        "p_maa_x": pk(inputs["L0_attn_time_maa_x"]),
        "p_maa_r": pk(inputs["L0_attn_time_maa_r"]),
        "p_maa_w": pk(inputs["L0_attn_time_maa_w"]),
        "p_maa_k": pk(inputs["L0_attn_time_maa_k"]),
        "p_maa_v": pk(inputs["L0_attn_time_maa_v"]),
        "p_maa_a": pk(inputs["L0_attn_time_maa_a"]),
        "p_maa_g": pk(inputs["L0_attn_time_maa_g"]),
        "p_ndecay": pk(-np.asarray(inputs["L0_attn_time_decay"],
                                   dtype=np.float32)),
        "p_aaaaa": pk(inputs["L0_attn_time_aaaaa"]),
        "p_faaaa": pk(inputs["L0_attn_time_faaaa"]),
        "p_ln_w": pk(inputs["L0_attn_ln_w"]),
        "p_ln_b": pk(inputs["L0_attn_ln_b"]),
        "p_ffn_g": pk(inputs["L0_ffn_ln_gamma"]),
        "p_ffn_b": pk(inputs["L0_ffn_ln_beta"]),
    }

    su = np.triu(np.ones((128, 64), np.float32), 1)[0:128, :]
    su = np.zeros((128, 128), np.float32)
    for r in range(128):
        for cpos in range(64):
            rr = r % 64
            su[r, cpos] = 1.0 if cpos > rr else 0.0
            su[r, 64 + cpos] = 1.0 if cpos >= rr else 0.0
    sl = np.zeros((128, 64), np.float32)
    for r in range(128):
        rr = r % 64
        sl[r, 0:rr] = 1.0
    idn = np.eye(128, dtype=np.float32)
    hd = np.zeros((128, PCH * 16), np.float32)
    hx = np.zeros((16, PCH * 128), np.float32)
    for q in range(PCH):
        for cpos in range(128):
            h = 2 * q + (cpos // 64)
            hd[cpos, q * 16 + h] = 1.0
            hx[h, q * 128 + cpos] = 1.0

    common = {
        "W_r": np.ascontiguousarray(inputs["L0_attn_W_r"], dtype=np.float32),
        "W_k": np.ascontiguousarray(inputs["L0_attn_W_k"], dtype=np.float32),
        "W_v": np.ascontiguousarray(inputs["L0_attn_W_v"], dtype=np.float32),
        "W_o": np.ascontiguousarray(inputs["L0_attn_W_o"], dtype=np.float32),
        "ffn_w": np.ascontiguousarray(inputs["L0_ffn_weight"],
                                      dtype=np.float32),
        "maa_w1": np.ascontiguousarray(inputs["L0_attn_time_maa_w1"],
                                       dtype=np.float32),
        "maa_w2": np.ascontiguousarray(inputs["L0_attn_time_maa_w2"],
                                       dtype=np.float32),
        "decay_w1": np.ascontiguousarray(inputs["L0_attn_time_decay_w1"],
                                         dtype=np.float32),
        "decay_w2": np.ascontiguousarray(inputs["L0_attn_time_decay_w2"],
                                         dtype=np.float32),
        "aaa_w1": np.ascontiguousarray(inputs["L0_attn_time_aaa_w1"],
                                       dtype=np.float32),
        "aaa_w2": np.ascontiguousarray(inputs["L0_attn_time_aaa_w2"],
                                       dtype=np.float32),
        "kkk_w1": np.ascontiguousarray(inputs["L0_attn_time_kkk_w1"],
                                       dtype=np.float32),
        "kkk_w2": np.ascontiguousarray(inputs["L0_attn_time_kkk_w2"],
                                       dtype=np.float32),
        "gate_w1": np.ascontiguousarray(inputs["L0_attn_gate_w1"],
                                        dtype=np.float32),
        "gate_w2": np.ascontiguousarray(inputs["L0_attn_gate_w2"],
                                        dtype=np.float32),
        "idn": idn, "msk_su": su, "msk_sl": sl, "hind": hd, "hexp": hx,
        "cvals": np.tile(np.array([[LN1_EPS, 1.0, -0.25, -0.5, -1.0,
                                    GN_EPS, LN2_EPS, 0.5]], np.float32),
                         (128, 1)),
    }
    common.update(par)

    in_maps = []
    for c in range(NCOR):
        m = dict(common)
        t0 = c * TC
        m["xT"] = np.ascontiguousarray(xTp[:, t0:t0 + TC + 1])
        s = np.zeros((1, 8), np.float32)
        s[0, :c] = 1.0
        m["sel"] = s
        in_maps.append(m)
    return in_maps


def kernel(**inputs):
    if "prog" not in _PROG_CACHE:
        _PROG_CACHE["prog"] = build_program()
    nc = _PROG_CACHE["prog"]
    in_maps = _host_inputs(inputs)
    res = run_bass_kernel_spmd(nc, in_maps, core_ids=list(range(NCOR)))
    outs = []
    for c in range(NCOR):
        o = np.asarray(res.results[c]["out"])
        outs.append(np.ascontiguousarray(o.T))
    full = np.concatenate(outs, axis=0)
    return full.reshape(1, T_ALL, C).astype(np.float32)
